# version 11
# baseline (speedup 1.0000x reference)
import numpy as np
import ml_dtypes
from contextlib import ExitStack

import concourse.bass as bass
import concourse.mybir as mybir
from concourse.bass_utils import run_bass_kernel_spmd

F32 = mybir.dt.float32
BF16 = mybir.dt.bfloat16
FP8 = mybir.dt.float8e4
AF = mybir.ActivationFunctionType
ALU = mybir.AluOpType
AX = mybir.AxisListType

T = 8192
D = 2048
NH = 4
DH = 128
RANK = 32
II = 6144
TQ = 2048
NT = 16
QW = 512
NKC = 64
NCI = 16
NIC = II // 128
NEC = D // 128
EPS = 1e-5
RSC = 1.0 / (128.0 ** 0.25)
MARGIN = 8.0
NEG = -1e30


class _DummyInstr:
    def then_inc(self, *a, **k):
        return self

    def annotate(self, *a, **k):
        return self


class _DummyEngine:
    def __getattr__(self, name):
        def f(*a, **k):
            return _DummyInstr()
        return f


_DUMMY = _DummyEngine()


class _EM:

    def __init__(self, blk, live):
        self._blk = blk
        self._live = live

    def _d(self, name, f):
        if name == self._live:
            getattr(self._blk, name)(f)
        else:
            f(_DUMMY)

    def sync(self, f):
        self._d("sync", f)

    def gpsimd(self, f):
        self._d("gpsimd", f)

    def vector(self, f):
        self._d("vector", f)

    def scalar(self, f):
        self._d("scalar", f)

    def tensor(self, f):
        self._d("tensor", f)


class _Alloc:
    def __init__(self):
        self.stacks = {}
        self.cache = {}
        self.ctr = {}
        self.first = True

    def begin_pass(self, first):
        self.first = first
        self.ctr = {}

    def enter_fn(self, sname):
        if self.first and sname not in self.stacks:
            self.stacks[sname] = ExitStack()

        def enter(cm, sname=sname):
            i = self.ctr.get(sname, 0)
            self.ctr[sname] = i + 1
            k = (sname, i)
            if self.first:
                self.cache[k] = self.stacks[sname].enter_context(cm)
            return self.cache[k]
        return enter

    def close(self, sname):
        if self.first:
            self.stacks[sname].close()



def build(dbg=()):
    nc = bass.Bass("TRN2", target_bir_lowering=False)

    dx = nc.dram_tensor("x_full", [T, D], F32, kind="ExternalInput").ap()
    dxo = nc.dram_tensor("x_own", [TQ, D], F32, kind="ExternalInput").ap()
    dwq = nc.dram_tensor("wq", [D, NH * DH], F32, kind="ExternalInput").ap()
    dwk = nc.dram_tensor("wk", [D, DH], F32, kind="ExternalInput").ap()
    dwv = nc.dram_tensor("wv", [D, DH], F32, kind="ExternalInput").ap()
    dwa = nc.dram_tensor("wa", [D, RANK], F32, kind="ExternalInput").ap()
    dwb = nc.dram_tensor("wb", [D, RANK], F32, kind="ExternalInput").ap()
    dwo = nc.dram_tensor("wo_my", [NH * DH, D], F32, kind="ExternalInput").ap()
    dgw = nc.dram_tensor("gate_w", [D, II], F32, kind="ExternalInput").ap()
    duw = nc.dram_tensor("up_w", [D, II], F32, kind="ExternalInput").ap()
    ddw = nc.dram_tensor("down_w", [II, D], F32, kind="ExternalInput").ap()
    dgs = nc.dram_tensor("gs_r", [128, NIC], F32, kind="ExternalInput").ap()
    dus = nc.dram_tensor("us_r", [128, NIC], F32, kind="ExternalInput").ap()
    dds = nc.dram_tensor("ds_r", [128, NEC], F32, kind="ExternalInput").ap()
    dpol = nc.dram_tensor("pol3", [1, 12], F32, kind="ExternalInput").ap()
    dcos = nc.dram_tensor("cos_t", [128, T], F32, kind="ExternalInput").ap()
    dsin = nc.dram_tensor("sin_t", [128, T], F32, kind="ExternalInput").ap()
    dident = nc.dram_tensor("ident", [128, 128], F32, kind="ExternalInput").ap()
    djio = nc.dram_tensor("jiota", [128, NKC], F32, kind="ExternalInput").ap()
    dqio = nc.dram_tensor("qiota", [1, QW], F32, kind="ExternalInput").ap()
    dmask = nc.dram_tensor("maskneg", [4 * 128, QW], F32, kind="ExternalInput").ap()
    dout = nc.dram_tensor("out", [TQ, D], F32, kind="ExternalOutput").ap()

    q_scr = nc.dram_tensor("q_scr", [128, NH * T], BF16)
    gw_scr = nc.dram_tensor("gw_scr", [NIC * 128, D], FP8)
    uw_scr = nc.dram_tensor("uw_scr", [NIC * 128, D], FP8)
    dw_scr = nc.dram_tensor("dw_scr", [NEC * 128, II], FP8)
    rs_in = nc.dram_tensor("rs_in", [T, D], F32)
    rs_out = nc.dram_tensor("rs_out", [TQ, D], F32)
    dn_scr = nc.dram_tensor("dn_scr", [TQ, D], F32)

    dbg_outs = {}
    for name, shape in (("xnfm", [128, NCI * 512]), ("q", [128, NH * T]),
                        ("k", [128, T]), ("v", [128, T]), ("a", [32, T]),
                        ("b", [32, T]), ("attf", [128, NH * QW]),
                        ("lrow", [128, QW]), ("woin", [T, D]),
                        ("xnown", [128, NCI * TQ]), ("fuse", [128, NIC * 1024])):
        if name in dbg:
            dbg_outs[name] = nc.dram_tensor("dbg_" + name, shape, F32,
                                            kind="ExternalOutput").ap()

    ctx = ExitStack()
    with ctx:
        e = ctx.enter_context

        pp = e(nc.psum_tensor("pp", [128, 4096], F32))

        def bank(i, p0=0, p1=128, c0=0, c1=512):
            return pp[p0:p1, 512 * i + c0:512 * i + c1]

        ident = e(nc.sbuf_tensor("ident_sb", [128, 128], BF16))
        identf = e(nc.sbuf_tensor("identf", [128, 128], F32))
        ones_bf = e(nc.sbuf_tensor("ones_bf", [128, 1], BF16))
        onesf = e(nc.sbuf_tensor("onesf", [128, 1], F32))
        onesrow_bf = e(nc.sbuf_tensor("onesrow_bf", [1, 128], BF16))
        onesrow_f = e(nc.sbuf_tensor("onesrow_f", [1, 128], F32))
        prm = e(nc.sbuf_tensor("prm", [1, 32], F32))
        t11 = e(nc.sbuf_tensor("t11", [1, 16], F32))
        cvec = e(nc.sbuf_tensor("cvec", [128, NH], F32))
        ghvec = e(nc.sbuf_tensor("ghvec", [128, NH], F32))

        sems = {}
        for name in ("sx", "sgd", "spe", "sve", "sac", "scc", "sout",
                     "srs", "sdbg", "sxb0", "sxb1", "sxb2", "sxb3", "scs0", "scs1",
                     "ssw0", "ssw1", "sqo0", "sqo1", "sqt0", "sqt1",
                     "sw50", "sw51", "sprep"):
            sems[name] = [e(nc.semaphore(name)), 0]

        def inc(instr, name, k=1):
            s = sems[name]
            instr.then_inc(s[0], k)
            s[1] += k
            return s[1]

        def cnt(name):
            return sems[name][1]

        def wait(eng, name, val):
            if val > 0:
                eng.wait_ge(sems[name][0], val)

        ALO = _Alloc()
        _finals = None
        _real_blk = nc.Block()
        _real_blk.__enter__()
        for _pi, _live in enumerate(("sync", "gpsimd", "vector", "scalar",
                                     "tensor")):
            ALO.begin_pass(_pi == 0)
            for _s in sems.values():
                _s[1] = 0
            blk = _EM(_real_blk, _live)

            def fence(exclude=("scc",)):
                snap = [(n, sems[n][1]) for n in sems if n not in exclude]

                def fw(eng, snap=snap):
                    for n, v in snap:
                        wait(eng, n, v)
                for _en in ("sync", "gpsimd", "vector", "scalar", "tensor"):
                    blk._d(_en, fw)

            eA = ALO.enter_fn("A")
            k_sb = eA(nc.sbuf_tensor("k_sb", [128, T], BF16))
            v_tm = eA(nc.sbuf_tensor("v_tm", [128, T], BF16))
            a_sb = eA(nc.sbuf_tensor("a_sb", [64, T], BF16))
            b_sb = eA(nc.sbuf_tensor("b_sb", [64, T], BF16))
            wq_sb = eA(nc.sbuf_tensor("wq_sb", [128, NCI * 512], BF16))
            wk_sb = eA(nc.sbuf_tensor("wk_sb", [128, NCI * 128], BF16))
            wv_sb = eA(nc.sbuf_tensor("wv_sb", [128, NCI * 128], BF16))
            wa_sb = eA(nc.sbuf_tensor("wa_sb", [128, NCI * 32], BF16))
            wb_sb = eA(nc.sbuf_tensor("wb_sb", [128, NCI * 32], BF16))

            if True:
                def g_loads(g):
                    inc(g.dma_start(out=ident[:], in_=dident), "sgd", 16)
                    inc(g.dma_start(out=identf[:], in_=dident), "sgd", 16)
                    for ci in range(NCI):
                        inc(g.dma_start(out=wq_sb[:, 512 * ci:512 * ci + 512],
                                        in_=dwq[128 * ci:128 * ci + 128, :]), "sgd", 16)
                        inc(g.dma_start(out=wk_sb[:, 128 * ci:128 * ci + 128],
                                        in_=dwk[128 * ci:128 * ci + 128, :]), "sgd", 16)
                        inc(g.dma_start(out=wv_sb[:, 128 * ci:128 * ci + 128],
                                        in_=dwv[128 * ci:128 * ci + 128, :]), "sgd", 16)
                        inc(g.dma_start(out=wa_sb[:, 32 * ci:32 * ci + 32],
                                        in_=dwa[128 * ci:128 * ci + 128, :]), "sgd", 16)
                        inc(g.dma_start(out=wb_sb[:, 32 * ci:32 * ci + 32],
                                        in_=dwb[128 * ci:128 * ci + 128, :]), "sgd", 16)
                blk.gpsimd(g_loads)

                def s_loads(s):
                    inc(s.dma_start(out=prm[0:1, 0:12], in_=dpol), "sx", 16)
                blk.sync(s_loads)

                def v_init(v):
                    wait(v, "sgd", cnt("sgd"))
                    wait(v, "sx", cnt("sx"))
                    v.memset(ones_bf[:], 1.0)
                    v.memset(onesf[:], 1.0)
                    v.memset(onesrow_bf[:], 1.0)
                    v.memset(onesrow_f[:], 1.0)
                    inc(v.tensor_scalar_min(t11[0:1, 0:4], prm[0:1, 0:4], 1.0), "sve")
                    wait(v, "sve", cnt("sve"))
                    inc(v.tensor_scalar_max(t11[0:1, 0:4], t11[0:1, 0:4], -1.0), "sve")
                    wait(v, "sve", cnt("sve"))
                    v.tensor_scalar_mul(t11[0:1, 0:4], t11[0:1, 0:4], 1.0 / 4096.0)
                    inc(v.tensor_scalar_mul(t11[0:1, 12:16], prm[0:1, 8:12], -1.0),
                        "sve")
                    wait(v, "sve", cnt("sve"))
                    inc(v.tensor_max(t11[0:1, 12:16], t11[0:1, 12:16],
                                     prm[0:1, 8:12]), "sve")
                blk.vector(v_init)

                def a_init(s):
                    wait(s, "sve", cnt("sve"))
                    inc(s.activation(t11[0:1, 4:8], t11[0:1, 12:16], AF.Exp,
                                     scale=-1.0), "sac")
                    wait(s, "sac", cnt("sac"))
                    inc(s.activation(t11[0:1, 8:12], t11[0:1, 4:8], AF.Ln,
                                     bias=1.0), "sac")
                    wait(s, "sac", cnt("sac"))
                    inc(s.activation(t11[0:1, 4:8], prm[0:1, 4:8], AF.Tanh,
                                     scale=0.5), "sac")
                blk.scalar(a_init)

                def v_init2(v):
                    wait(v, "sac", cnt("sac"))
                    inc(v.tensor_scalar_max(t11[0:1, 12:16], prm[0:1, 8:12], 0.0),
                        "sve")
                    wait(v, "sve", cnt("sve"))
                    inc(v.tensor_add(t11[0:1, 8:12], t11[0:1, 8:12],
                                     t11[0:1, 12:16]), "sve")
                    wait(v, "sve", cnt("sve"))
                    inc(v.tensor_scalar_max(t11[0:1, 8:12], t11[0:1, 8:12], 1e-6),
                        "sve")
                    wait(v, "sve", cnt("sve"))
                    inc(v.tensor_add(prm[0:1, 12:16], t11[0:1, 0:4], t11[0:1, 8:12]),
                        "sve")
                    wait(v, "sve", cnt("sve"))
                    v.tensor_scalar(prm[0:1, 16:20], t11[0:1, 4:8], 0.25, 0.25,
                                    ALU.mult, ALU.add)
                    inc(v.tensor_scalar_mul(prm[0:1, 20:24], prm[0:1, 12:16], -1.0),
                        "sve")
                blk.vector(v_init2)

                def t_bcast(t):
                    wait(t, "sve", cnt("sve"))
                    last = None
                    for h in range(NH):
                        t.matmul(bank(0, c0=h, c1=h + 1), onesrow_f[:],
                                 prm[0:1, 12 + h:13 + h], start=True, stop=True)
                        last = t.matmul(bank(0, c0=8 + h, c1=9 + h), onesrow_f[:],
                                        prm[0:1, 16 + h:17 + h], start=True, stop=True)
                    inc(last, "spe")
                blk.tensor(t_bcast)

                def a_bcast(s):
                    wait(s, "spe", cnt("spe"))
                    s.copy(cvec[:], bank(0, c0=0, c1=4))
                    inc(s.copy(ghvec[:], bank(0, c0=8, c1=12)), "sac")
                blk.scalar(a_bcast)

            fence()

            def emit_ln_phase(blk, src_ap, n_tiles, fm_dst_fn, xblk, xn_tm, stat,
                              rope_cfg):
                xfree = [[0, 0] for _ in range(4)]
                prev_stats = [0]
                tile_tail_sac = 0
                tr_prev = 0

                for t in range(n_tiles):
                    for b in range(4):
                        gi = 4 * t + b

                        def dma_x(s, gi=gi, i=b):
                            wait(s, "sac", xfree[i][0])
                            wait(s, "sve", xfree[i][1])
                            inc(s.dma_start(out=xblk[i][:],
                                            in_=src_ap[128 * gi:128 * gi + 128, :]),
                                f"sxb{i}", 16)
                        blk.sync(dma_x)

                    if rope_cfg is not None:
                        cs = rope_cfg
                        slot = t % 2

                        def dma_cs(s, t=t, slot=slot):
                            wait(s, "sve", cs["cs_free"][slot])
                            inc(s.dma_start(out=cs["cos_sb"][:, 512 * slot:512 * slot + 512],
                                            in_=dcos[:, 512 * t:512 * t + 512]),
                                f"scs{slot}", 16)
                            inc(s.dma_start(out=cs["sin_sb"][:, 512 * slot:512 * slot + 512],
                                            in_=dsin[:, 512 * t:512 * t + 512]),
                                f"scs{slot}", 16)
                        blk.sync(dma_cs)
                        cs["cs_ready"][slot] = cnt(f"scs{slot}")

                    sq_counts = []
                    red_counts = []
                    for b in range(4):
                        gi = 4 * t + b
                        i = b
                        xneed = cnt(f"sxb{i}")

                        def act_sq(s, i=i, b=b, xneed=xneed,
                                   pv=prev_stats[0], tr_prev=tr_prev):
                            wait(s, f"sxb{i}", xneed)
                            wait(s, "sve", pv)
                            wait(s, "spe", tr_prev)
                            inc(s.activation(xn_tm[:, D * b:D * b + D], xblk[i][:],
                                             AF.Square,
                                             accum_out=stat[:, 4 + b:5 + b]), "sac")
                        blk.scalar(act_sq)
                        sq_counts.append(cnt("sac"))

                        def ve_red(v, i=i, b=b, xneed=xneed, pv=prev_stats[0]):
                            wait(v, f"sxb{i}", xneed)
                            wait(v, "sve", pv)
                            inc(v.reduce_sum(stat[:, b:b + 1], xblk[i][:],
                                             axis=AX.X), "sve")
                        blk.vector(ve_red)
                        red_counts.append(cnt("sve"))

                    def ve_chain(v, sq_all=sq_counts[-1], red_all=red_counts[-1]):
                        wait(v, "sac", sq_all)
                        wait(v, "sve", red_all)
                        inc(v.tensor_scalar_mul(stat[:, 8:12], stat[:, 0:4],
                                                1.0 / D), "sve")
                        wait(v, "sve", cnt("sve"))
                        inc(v.tensor_mul(stat[:, 12:16], stat[:, 8:12],
                                         stat[:, 8:12]), "sve")
                        wait(v, "sve", cnt("sve"))
                        inc(v.scalar_tensor_tensor(stat[:, 4:8], stat[:, 4:8],
                                                   1.0 / D, stat[:, 12:16],
                                                   ALU.mult, ALU.subtract), "sve")
                        wait(v, "sve", cnt("sve"))
                        inc(v.tensor_scalar_add(stat[:, 4:8], stat[:, 4:8], EPS),
                            "sve")
                        wait(v, "sve", cnt("sve"))
                        inc(v.reciprocal(stat[:, 12:16], stat[:, 4:8]), "sve")
                    blk.vector(ve_chain)
                    rsq_done = cnt("sve")

                    def act_sqrt(s, rsq_done=rsq_done):
                        wait(s, "sve", rsq_done)
                        inc(s.activation(stat[:, 16:20], stat[:, 12:16], AF.Sqrt),
                            "sac")
                    blk.scalar(act_sqrt)
                    rstd_done = cnt("sac")

                    def ve_mr(v, rstd_done=rstd_done):
                        wait(v, "sac", rstd_done)
                        inc(v.scalar_tensor_tensor(stat[:, 0:4], stat[:, 8:12], -1.0,
                                                   stat[:, 16:20], ALU.mult,
                                                   ALU.mult), "sve")
                    blk.vector(ve_mr)
                    mr_done = cnt("sve")

                    for b in range(4):
                        i = b

                        def ve_xn(v, i=i, b=b, mr_done=mr_done):
                            wait(v, "sve", mr_done)
                            inc(v.tensor_scalar(xn_tm[:, D * b:D * b + D], xblk[i][:],
                                                stat[:, 16 + b:17 + b],
                                                stat[:, b:b + 1],
                                                ALU.mult, ALU.add), "sve")
                        blk.vector(ve_xn)
                        xfree[i] = [sq_counts[b], cnt("sve")]
                    prev_stats[0] = cnt("sve")
                    xn_ready = cnt("sve")

                    evac_done = {}
                    for ci in range(NCI):
                        tb = 6 + (ci % 2)
                        prev = evac_done.get(ci - 2, tile_tail_sac)

                        def pe_tr(t_, ci=ci, tb=tb, prev=prev, xn_ready=xn_ready):
                            wait(t_, "sve", xn_ready)
                            wait(t_, "sac", prev)
                            last = None
                            for b in range(4):
                                last = t_.matmul(
                                    bank(tb, c0=128 * b, c1=128 * b + 128),
                                    xn_tm[:, D * b + 128 * ci:D * b + 128 * ci + 128],
                                    ident[:], start=True, stop=True)
                            inc(last, "spe")
                        blk.tensor(pe_tr)
                        tr_done = cnt("spe")
                        if ci == NCI - 1:
                            tr_prev = tr_done

                        def act_ev(s, ci=ci, tb=tb, tr_done=tr_done, t=t):
                            wait(s, "spe", tr_done)
                            inc(s.copy(fm_dst_fn(ci, t), bank(tb)), "sac")
                        blk.scalar(act_ev)
                        evac_done[ci] = cnt("sac")

                    if rope_cfg is None:
                        tile_tail_sac = cnt("sac")
                        continue

                    cs = rope_cfg
                    all_evac = cnt("sac")
                    xn_fm = cs["xn_fm"]

                    def pe_proj(t_, t=t, all_evac=all_evac,
                                pb=cs["proj_bank_free"][0]):
                        wait(t_, "sac", all_evac)
                        wait(t_, "sac", pb)
                        for ci in range(NCI):
                            st = ci == 0
                            sp = ci == NCI - 1
                            rhs = xn_fm[:, 512 * ci:512 * ci + 512]
                            for qh in range(4):
                                t_.matmul(bank(qh),
                                          wq_sb[:, 512 * ci + 128 * qh:
                                                512 * ci + 128 * qh + 128],
                                          rhs, start=st, stop=sp)
                            t_.matmul(bank(4), wk_sb[:, 128 * ci:128 * ci + 128], rhs,
                                      start=st, stop=sp)
                            t_.matmul(bank(5), wv_sb[:, 128 * ci:128 * ci + 128], rhs,
                                      start=st, stop=sp)
                            t_.matmul(bank(6, p0=32, p1=64),
                                      wa_sb[:, 32 * ci:32 * ci + 32], rhs,
                                      start=st, stop=sp)
                            last = t_.matmul(bank(7, p0=32, p1=64),
                                             wb_sb[:, 32 * ci:32 * ci + 32], rhs,
                                             start=st, stop=sp)
                        inc(last, "spe")
                    blk.tensor(pe_proj)
                    proj_done = cnt("spe")

                    slot = t % 2
                    qst = cs["qstage"][slot]
                    for j in range(5):
                        pair = j % 2
                        qf = cs["qf32"][pair]
                        qs2 = cs["qswap"][pair]

                        def act_evq(s, j=j, qf=qf, proj_done=proj_done,
                                    pv=cs["pair_free"][pair]):
                            wait(s, "spe", proj_done)
                            wait(s, "sve", pv)
                            inc(s.copy(qf[:], bank(j)), "sac")
                        blk.scalar(act_evq)
                        ev_done = cnt("sac")

                        def g_swap(g, qf=qf, qs2=qs2, ev_done=ev_done, pair=pair):
                            wait(g, "sac", ev_done)
                            inc(g.dma_start(out=qs2[0:64, :], in_=qf[64:128, :]),
                                f"ssw{pair}", 16)
                            inc(g.dma_start(out=qs2[64:128, :], in_=qf[0:64, :]),
                                f"ssw{pair}", 16)
                        blk.gpsimd(g_swap)
                        sw_done = cnt(f"ssw{pair}")

                        dst = (qst[:, 512 * j:512 * j + 512] if j < 4
                               else k_sb[:, 512 * t:512 * t + 512])

                        def ve_rope(v, qf=qf, qs2=qs2, sw_done=sw_done, dst=dst,
                                    slot=slot, qw=cs["qst_free"][slot], pair=pair):
                            wait(v, f"ssw{pair}", sw_done)
                            wait(v, f"scs{slot}", cs["cs_ready"][slot])
                            wait(v, f"sqo{slot}", qw)
                            v.tensor_mul(qs2[:], qs2[:],
                                         cs["sin_sb"][:, 512 * slot:512 * slot + 512])
                            inc(v.tensor_mul(qf[:], qf[:],
                                             cs["cos_sb"][:, 512 * slot:512 * slot + 512]),
                                "sve")
                            wait(v, "sve", cnt("sve"))
                            inc(v.tensor_add(dst, qf[:], qs2[:]), "sve")
                        blk.vector(ve_rope)
                        cs["pair_free"][pair] = cnt("sve")
                    cs["cs_free"][slot] = cnt("sve")
                    rope_all = cnt("sve")

                    def g_qout(g, t=t, qst=qst, rope_all=rope_all, slot=slot):
                        wait(g, "sve", rope_all)
                        for j in range(4):
                            inc(g.dma_start(out=q_scr.ap()[:, T * j + 512 * t:
                                                           T * j + 512 * t + 512],
                                            in_=qst[:, 512 * j:512 * j + 512]),
                                f"sqo{slot}", 16)
                    blk.gpsimd(g_qout)
                    cs["qst_free"][slot] = cnt(f"sqo{slot}")

                    def act_evv(s, proj_done=proj_done):
                        wait(s, "spe", proj_done)
                        inc(s.copy(cs["vtmp"][:], bank(5)), "sac")
                    blk.scalar(act_evv)
                    vtmp_done = cnt("sac")

                    def pe_vtr(t_, vtmp_done=vtmp_done):
                        wait(t_, "sac", vtmp_done)
                        last = None
                        for jj in range(4):
                            last = t_.matmul(bank(5, c0=128 * jj, c1=128 * jj + 128),
                                             cs["vtmp"][:, 128 * jj:128 * jj + 128],
                                             ident[:], start=True, stop=True)
                        inc(last, "spe")
                    blk.tensor(pe_vtr)
                    vtr_done = cnt("spe")

                    def act_evv2(s, t=t, vtr_done=vtr_done):
                        wait(s, "spe", vtr_done)
                        s.copy(v_tm[:, 512 * t:512 * t + 512], bank(5))
                        s.copy(a_sb[32:64, 512 * t:512 * t + 512],
                               bank(6, p0=32, p1=64))
                        inc(s.copy(b_sb[32:64, 512 * t:512 * t + 512],
                                   bank(7, p0=32, p1=64)), "sac")
                    blk.scalar(act_evv2)
                    tile_tail_sac = cnt("sac")
                    cs["proj_bank_free"][0] = cnt("sac")

            e1 = ALO.enter_fn("c1")
            xblk = [e1(nc.sbuf_tensor(f"xblk{i}", [128, D], F32)) for i in range(4)]
            xn_tm = e1(nc.sbuf_tensor("xn_tm", [128, 4 * D], BF16))
            xn_fm = e1(nc.sbuf_tensor("xn_fm", [128, NCI * 512], BF16))
            stat = e1(nc.sbuf_tensor("stat", [128, 20], F32))
            cos_sb = e1(nc.sbuf_tensor("cos_sb", [128, 1024], F32))
            sin_sb = e1(nc.sbuf_tensor("sin_sb", [128, 1024], F32))
            qf32 = [e1(nc.sbuf_tensor(f"qf32_{i}", [128, 512], F32)) for i in range(2)]
            qswap = [e1(nc.sbuf_tensor(f"qswap{i}", [128, 512], F32)) for i in range(2)]
            vtmp = e1(nc.sbuf_tensor("vtmp", [128, 512], BF16))
            qstage = [e1(nc.sbuf_tensor(f"qstage{i}", [128, 4 * 512], BF16))
                      for i in range(2)]

            rope_cfg = {"cos_sb": cos_sb, "sin_sb": sin_sb, "qf32": qf32,
                        "qswap": qswap, "vtmp": vtmp, "xn_fm": xn_fm,
                        "qstage": qstage, "qst_free": [0, 0],
                        "cs_ready": [0, 0], "cs_free": [0, 0],
                        "pair_free": [0, 0], "proj_bank_free": [0]}

            if True:
                emit_ln_phase(blk, dx, NT, lambda ci, t: xn_fm[:, 512 * ci:512 * ci + 512],
                              xblk, xn_tm, stat, rope_cfg)
                if "xnfm" in dbg:
                    def dmp(s):
                        wait(s, "sac", cnt("sac"))
                        inc(s.dma_start(out=dbg_outs["xnfm"], in_=xn_fm[:]), "sdbg", 16)
                    blk.gpsimd(dmp)

            ALO.close("c1")
            fence()

            eA2 = ALO.enter_fn("A2")
            qt_sb = [eA2(nc.sbuf_tensor(f"qt_sb{i}", [128, NH * 512], BF16))
                     for i in range(2)]
            wo_sb = eA2(nc.sbuf_tensor("wo_sb", [128, NH * D], BF16))
            jio = eA2(nc.sbuf_tensor("jio", [128, NKC], F32))
            qio = eA2(nc.sbuf_tensor("qio", [1, QW], F32))
            maskneg = eA2(nc.sbuf_tensor("maskneg_sb", [128, 4 * QW], BF16))
            s2vec = eA2(nc.sbuf_tensor("s2vec", [128, NH], F32))
            bias_blk = eA2(nc.sbuf_tensor("bias_blk", [128, NH * NKC], F32))
            rowq = eA2(nc.sbuf_tensor("rowq", [1, NH * QW], BF16))
            mt_sb = eA2(nc.sbuf_tensor("mt_sb", [128, 2 * QW], BF16))
            p_sb = eA2(nc.sbuf_tensor("p_sb", [128, 2 * NH * QW], BF16))
            pv_bf = eA2(nc.sbuf_tensor("pv_bf", [128, QW], BF16))
            att_tm = eA2(nc.sbuf_tensor("att_tm", [128, QW], BF16))
            attf = eA2(nc.sbuf_tensor("attf", [128, NH * QW], BF16))
            l_sb = eA2(nc.sbuf_tensor("l_sb", [128, QW], F32))
            linv = eA2(nc.sbuf_tensor("linv", [128, 4 * NH], F32))
            wo_out = eA2(nc.sbuf_tensor("wo_out", [128, D], F32))
            sq512 = eA2(nc.sbuf_tensor("sq512", [128, 512], BF16))
            krow = eA2(nc.sbuf_tensor("krow", [1, 512], F32))
            ghB = eA2(nc.sbuf_tensor("ghB", [1, NH], F32))

            if True:
                def g_loads2(g):
                    wait(g, "sout", cnt("sout"))
                    wait(g, "sdbg", cnt("sdbg"))
                    wait(g, "sqo0", cnt("sqo0"))
                    wait(g, "sqo1", cnt("sqo1"))
                    for hc in range(NH):
                        inc(g.dma_start(out=wo_sb[:, D * hc:D * hc + D],
                                        in_=dwo[128 * hc:128 * hc + 128, :]), "sgd", 16)
                    for r in range(4):
                        inc(g.dma_start(out=maskneg[:, QW * r:QW * r + QW],
                                        in_=dmask[128 * r:128 * r + 128, :]), "sgd", 16)
                blk.gpsimd(g_loads2)

                def s_loads2(s):
                    wait(s, "sout", cnt("sout"))
                    wait(s, "sdbg", cnt("sdbg"))
                    wait(s, "sqo0", cnt("sqo0"))
                    wait(s, "sqo1", cnt("sqo1"))
                    inc(s.dma_start(out=jio[:], in_=djio), "sx", 16)
                    inc(s.dma_start(out=qio[:], in_=dqio), "sx", 16)
                blk.sync(s_loads2)

                def v_rowq(v):
                    wait(v, "sgd", cnt("sgd"))
                    wait(v, "sx", cnt("sx"))
                    wait(v, "sout", cnt("sout"))
                    wait(v, "sdbg", cnt("sdbg"))
                    wait(v, "sqo0", cnt("sqo0"))
                    wait(v, "sqo1", cnt("sqo1"))
                    last = None
                    for h in range(NH):
                        last = v.tensor_scalar_mul(rowq[0:1, QW * h:QW * h + QW],
                                                   qio[0:1, :],
                                                   prm[0:1, 20 + h:21 + h])
                    inc(last, "sve")
                blk.vector(v_rowq)
            fence()

            if any(k in dbg for k in ("q", "k", "v", "a", "b")):
                if True:
                    def dmp(s):
                        wait(s, "sqo0", cnt("sqo0"))
                        wait(s, "sqo1", cnt("sqo1"))
                        if "q" in dbg:
                            inc(s.dma_start(out=dbg_outs["q"], in_=q_scr.ap()),
                                "sdbg", 16)
                        for name, srct in (("k", k_sb[:, :]), ("v", v_tm[:, :]),
                                           ("a", a_sb[32:64, :]),
                                           ("b", b_sb[32:64, :])):
                            if name in dbg:
                                inc(s.dma_start(out=dbg_outs[name], in_=srct),
                                    "sdbg", 16)
                    blk.gpsimd(dmp)
                fence()

            if True:
                def g_prep(g):
                    for ic in range(NIC):
                        gsrc = bass.AP(dgw.tensor, 128 * ic,
                                       [[II, 128], [128 * II, NCI], [1, 128]])
                        gdst = bass.AP(gw_scr, 128 * ic * D,
                                       [[D, 128], [128, NCI], [1, 128]])
                        inc(g.dma_start(out=gdst, in_=gsrc), "sprep", 16)
                        usrc = bass.AP(duw.tensor, 128 * ic,
                                       [[II, 128], [128 * II, NCI], [1, 128]])
                        udst = bass.AP(uw_scr, 128 * ic * D,
                                       [[D, 128], [128, NCI], [1, 128]])
                        inc(g.dma_start(out=udst, in_=usrc), "sprep", 16)
                    for ec in range(NEC):
                        dsrc = bass.AP(ddw.tensor, 128 * ec,
                                       [[D, 128], [128 * D, NIC], [1, 128]])
                        ddst = bass.AP(dw_scr, 128 * ec * II,
                                       [[II, 128], [128, NIC], [1, 128]])
                        inc(g.dma_start(out=ddst, in_=dsrc), "sprep", 16)
                blk.gpsimd(g_prep)

                def ve_z(v):
                    inc(v.memset(prm[0:1, 24:29], 0.0), "sve")
                blk.vector(ve_z)

                def norms_from(src_fn, dstcol, need_dma):
                    for s16 in range(16):
                        bnk = 2 + (s16 % 2)
                        if need_dma:
                            def sy_q(s, s16=s16, prev=cnt("sve"),
                                     q0=cnt("sqo0"), q1=cnt("sqo1")):
                                wait(s, "sve", prev)
                                wait(s, "sqo0", q0)
                                wait(s, "sqo1", q1)
                                inc(s.dma_start(out=qt_sb[0][:, 0:512],
                                                in_=src_fn(s16)), "sqt0", 16)
                            blk.sync(sy_q)
                            xr = cnt("sqt0")

                            def ve_sq(v, xr=xr):
                                wait(v, "sqt0", xr)
                                inc(v.tensor_mul(sq512[:], qt_sb[0][:, 0:512],
                                                 qt_sb[0][:, 0:512]), "sve")
                            blk.vector(ve_sq)
                        else:
                            def ve_sq(v, s16=s16, prev_pe=cnt("spe")):
                                wait(v, "spe", prev_pe)
                                inc(v.tensor_mul(sq512[:], src_fn(s16),
                                                 src_fn(s16)), "sve")
                            blk.vector(ve_sq)
                        sq_done = cnt("sve")

                        def pe_n(t_, bnk=bnk, sq_done=sq_done, prev_ev=cnt("sac")):
                            wait(t_, "sve", sq_done)
                            wait(t_, "sac", prev_ev)
                            inc(t_.matmul(bank(bnk, p0=0, p1=1), ones_bf[:],
                                          sq512[:], start=True, stop=True), "spe")
                        blk.tensor(pe_n)
                        mm_done = cnt("spe")

                        def act_n(s, bnk=bnk, mm_done=mm_done, prev_ve=cnt("sve")):
                            wait(s, "spe", mm_done)
                            wait(s, "sve", prev_ve)
                            inc(s.copy(krow[0:1, :], bank(bnk, p0=0, p1=1)), "sac")
                        blk.scalar(act_n)
                        ev_done = cnt("sac")

                        def ve_mx(v, dstcol=dstcol, ev_done=ev_done):
                            wait(v, "sac", ev_done)
                            inc(v.reduce_max(t11[0:1, 15:16], krow[0:1, :],
                                             axis=AX.X), "sve")
                            wait(v, "sve", cnt("sve"))
                            inc(v.tensor_max(prm[0:1, dstcol:dstcol + 1],
                                             prm[0:1, dstcol:dstcol + 1],
                                             t11[0:1, 15:16]), "sve")
                        blk.vector(ve_mx)

                norms_from(lambda s16: k_sb[:, 512 * s16:512 * s16 + 512], 28, False)
                for h in range(NH):
                    norms_from(lambda s16, h=h: q_scr.ap()[:, T * h + 512 * s16:
                                                           T * h + 512 * s16 + 512],
                               24 + h, True)

                def ve_ghb(v):
                    wait(v, "sve", cnt("sve"))
                    inc(v.tensor_scalar_add(t11[0:1, 0:4], prm[0:1, 24:28],
                                            prm[0:1, 28:29]), "sve")
                    wait(v, "sve", cnt("sve"))
                    inc(v.tensor_scalar(t11[0:1, 0:4], t11[0:1, 0:4], -0.5, -MARGIN,
                                        ALU.mult, ALU.add), "sve")
                    wait(v, "sve", cnt("sve"))
                    inc(v.tensor_add(ghB[0:1, 0:4], t11[0:1, 0:4], prm[0:1, 16:20]),
                        "sve")
                blk.vector(ve_ghb)

            fence()

            if True:
                b_sc = ((0, 1), (2, 3))
                exp_free = [0, 0, 0, 0]
                mt_cons = [0, 0]
                mtb_ac = 0
                mtb_ve = 0
                p_freed = [0, 0]
                l_evac = 0
                wo_prev = 0
                prev_s2b = 0
                qt_ready = [0, 0]
                qt_users = [0, 0]

                for t in range(NT):
                    nkc = 4 * (t + 1)
                    tb2 = t % 2

                    def sy_qt(s, t=t, tb2=tb2, qu=qt_users[tb2],
                              q0=cnt("sqo0"), q1=cnt("sqo1")):
                        wait(s, "spe", qu)
                        wait(s, "sqo0", q0)
                        wait(s, "sqo1", q1)
                        for h in range(NH):
                            inc(s.dma_start(out=qt_sb[tb2][:, 512 * h:512 * h + 512],
                                            in_=q_scr.ap()[:, T * h + 512 * t:
                                                           T * h + 512 * t + 512]),
                                f"sqt{tb2}", 16)
                    blk.sync(sy_qt)
                    qt_ready[tb2] = cnt(f"sqt{tb2}")

                    def ve_s2(v, t=t, prev_s2b=prev_s2b):
                        wait(v, "spe", prev_s2b)
                        last = None
                        for h in range(NH):
                            last = v.tensor_scalar(t11[0:1, h:h + 1],
                                                   prm[0:1, 12 + h:13 + h],
                                                   float(-512 * t), ghB[0:1, h:h + 1],
                                                   ALU.mult, ALU.add)
                        inc(last, "sve")
                    blk.vector(ve_s2)
                    s2_done = cnt("sve")

                    def pe_s2(t_, s2_done=s2_done, ma=mtb_ac, mv=mtb_ve,
                              wp=wo_prev):
                        wait(t_, "sve", s2_done)
                        wait(t_, "sac", ma)
                        wait(t_, "sve", mv)
                        wait(t_, "sac", wp)
                        last = None
                        for h in range(NH):
                            last = t_.matmul(bank(4, c0=16 + h, c1=17 + h),
                                             onesrow_f[:], t11[0:1, h:h + 1],
                                             start=True, stop=True)
                        inc(last, "spe")
                    blk.tensor(pe_s2)
                    s2b_done = prev_s2b = cnt("spe")

                    def act_s2(s, s2b_done=s2b_done):
                        wait(s, "spe", s2b_done)
                        inc(s.copy(s2vec[:], bank(4, c0=16, c1=20)), "sac")
                    blk.scalar(act_s2)
                    s2v_done = cnt("sac")
                    mtb_ac = s2v_done

                    def ve_bias(v, s2v_done=s2v_done):
                        wait(v, "sac", s2v_done)
                        last = None
                        for h in range(NH):
                            last = v.tensor_scalar(bias_blk[:, NKC * h:NKC * h + NKC],
                                                   jio[:], cvec[:, h:h + 1],
                                                   s2vec[:, h:h + 1],
                                                   ALU.mult, ALU.add)
                        inc(last, "sve")
                    blk.vector(ve_bias)

                    for hp in range(2):
                        exp_done_k = {}

                        def emit_pv(kc, hp=hp, t=t, nkc=nkc):
                            par = kc % 2
                            h0, h1 = 2 * hp, 2 * hp + 1

                            def pe_pv(t_, t=t, kc=kc, par=par, h0=h0, h1=h1,
                                      nkc=nkc, ex=exp_done_k[kc], wp=wo_prev,
                                      le=l_evac):
                                wait(t_, "sac", ex)
                                if kc == 0:
                                    wait(t_, "sac", wp)
                                    wait(t_, "sac", le)
                                st = kc == 0
                                sp = kc == nkc - 1
                                for pi, h in ((0, h0), (1, h1)):
                                    t_.matmul(bank(5 + pi),
                                              v_tm[:, 128 * kc:128 * kc + 128],
                                              p_sb[:, QW * (2 * par + pi):
                                                   QW * (2 * par + pi) + QW],
                                              start=st, stop=sp)
                                last = None
                                for pi, h in ((0, h0), (1, h1)):
                                    last = t_.matmul(
                                        bank(7, p0=32 * h, p1=32 * h + 1),
                                        ones_bf[:],
                                        p_sb[:, QW * (2 * par + pi):
                                             QW * (2 * par + pi) + QW],
                                        start=st, stop=sp,
                                        tile_position=(0, 32 * h))
                                inc(last, "spe")
                            blk.tensor(pe_pv)
                            p_freed[par] = cnt("spe")

                        for kc in range(nkc):
                            par = kc % 2
                            sb0, sb1 = b_sc[par]
                            h0, h1 = 2 * hp, 2 * hp + 1

                            def pe_mt(t_, t=t, kc=kc, ma=mtb_ac, mv=mtb_ve):
                                wait(t_, "sac", ma)
                                if kc == 0:
                                    wait(t_, "sve", mv)
                                inc(t_.matmul(bank(4),
                                              b_sb[32:64, 128 * kc:128 * kc + 128],
                                              a_sb[32:64, 512 * t:512 * t + 512],
                                              start=True, stop=True), "spe")
                            blk.tensor(pe_mt)
                            mtmm_done = cnt("spe")

                            def act_mt(s, par=par, mtmm_done=mtmm_done,
                                       mtc=mt_cons[par]):
                                wait(s, "spe", mtmm_done)
                                wait(s, "sve", mtc)
                                inc(s.activation(mt_sb[:, QW * par:QW * par + QW],
                                                 bank(4), AF.Tanh, scale=0.5), "sac")
                            blk.scalar(act_mt)
                            mtb_ac = tanh_done = cnt("sac")

                            mask_done = 0
                            diag_r = kc - 4 * t
                            if diag_r >= 0:
                                def ve_mask(v, par=par, diag_r=diag_r,
                                            tanh_done=tanh_done):
                                    wait(v, "sac", tanh_done)
                                    inc(v.tensor_add(
                                        mt_sb[:, QW * par:QW * par + QW],
                                        mt_sb[:, QW * par:QW * par + QW],
                                        maskneg[:, QW * diag_r:QW * diag_r + QW]),
                                        "sve")
                                blk.vector(ve_mask)
                                mask_done = cnt("sve")

                            def pe_qk(t_, t=t, kc=kc, h0=h0, h1=h1, sb0=sb0,
                                      sb1=sb1, e0=exp_free[sb0], e1=exp_free[sb1],
                                      tb2=tb2, qr=qt_ready[tb2]):
                                wait(t_, "sac", max(e0, e1))
                                if kc < 2:
                                    wait(t_, f"sqt{tb2}", qr)
                                for bi, h in ((sb0, h0), (sb1, h1)):
                                    t_.matmul(bank(bi),
                                              k_sb[:, 128 * kc:128 * kc + 128],
                                              qt_sb[tb2][:, 512 * h:512 * h + 512],
                                              start=True, stop=False)
                                last = None
                                for bi, h in ((sb0, h0), (sb1, h1)):
                                    last = t_.matmul(bank(bi), onesrow_bf[:],
                                                     rowq[0:1, QW * h:QW * h + QW],
                                                     start=False, stop=True)
                                inc(last, "spe")
                            blk.tensor(pe_qk)
                            qk_done = cnt("spe")

                            def ve_stt(v, par=par, h0=h0, h1=h1, sb0=sb0, sb1=sb1,
                                       qk_done=qk_done, mra=tanh_done,
                                       mrv=mask_done):
                                wait(v, "spe", qk_done)
                                wait(v, "sac", mra)
                                wait(v, "sve", mrv)
                                last = None
                                for bi, h in ((sb0, h0), (sb1, h1)):
                                    last = v.scalar_tensor_tensor(
                                        bank(bi), mt_sb[:, QW * par:QW * par + QW],
                                        ghvec[:, h:h + 1], bank(bi),
                                        ALU.mult, ALU.add)
                                inc(last, "sve")
                            blk.vector(ve_stt)
                            stt_done = cnt("sve")
                            mt_cons[par] = stt_done

                            def act_exp(s, par=par, h0=h0, h1=h1, sb0=sb0, sb1=sb1,
                                        kc=kc, stt_done=stt_done, pf=p_freed[par]):
                                wait(s, "sve", stt_done)
                                wait(s, "spe", pf)
                                last = None
                                for bi, (pi, h) in ((sb0, (0, h0)), (sb1, (1, h1))):
                                    last = s.activation(
                                        p_sb[:, QW * (2 * par + pi):
                                             QW * (2 * par + pi) + QW],
                                        bank(bi), AF.Exp,
                                        bias=bias_blk[:, NKC * h + kc:
                                                      NKC * h + kc + 1])
                                inc(last, "sac")
                            blk.scalar(act_exp)
                            exp_free[sb0] = exp_free[sb1] = cnt("sac")
                            exp_done_k[kc] = cnt("sac")
                            if kc >= 1:
                                emit_pv(kc - 1)
                        emit_pv(nkc - 1)

                        pv_done = cnt("spe")
                        prev_pe = pv_done

                        for hi in range(2):
                            h = 2 * hp + hi

                            def act_ev1(s, h=h, hi=hi, prev_pe=prev_pe):
                                wait(s, "spe", prev_pe)
                                s.copy(pv_bf[:], bank(5 + hi))
                                inc(s.copy(l_sb[32 * h:32 * h + 1, :],
                                           bank(7, p0=32 * h, p1=32 * h + 1)),
                                    "sac")
                            blk.scalar(act_ev1)
                            ev1_done = cnt("sac")
                            l_evac = ev1_done

                            def pe_lt(t_, h=h, ev1_done=ev1_done, mv=mtb_ve,
                                      e0=exp_free[0]):
                                wait(t_, "sac", ev1_done)
                                wait(t_, "sve", mv)
                                wait(t_, "sac", e0)
                                for qc in range(4):
                                    t_.matmul(bank(4, c0=4 * h + qc,
                                                   c1=4 * h + qc + 1),
                                              l_sb[32 * h:32 * h + 1,
                                                   128 * qc:128 * qc + 128],
                                              onesf[32 * h:32 * h + 1, :],
                                              start=True, stop=True,
                                              tile_position=(32 * h, 0))
                                last = None
                                for qc in range(4):
                                    last = t_.matmul(
                                        bank(0, c0=128 * qc, c1=128 * qc + 128),
                                        pv_bf[:, 128 * qc:128 * qc + 128],
                                        ident[:], start=True, stop=True)
                                inc(last, "spe")
                            blk.tensor(pe_lt)
                            lt_done = prev_pe = cnt("spe")

                            def ve_linv(v, h=h, lt_done=lt_done):
                                wait(v, "spe", lt_done)
                                inc(v.reciprocal(linv[:, 4 * h:4 * h + 4],
                                                 bank(4, c0=4 * h, c1=4 * h + 4)),
                                    "sve")
                            blk.vector(ve_linv)
                            linv_done = cnt("sve")
                            mtb_ve = linv_done

                            def act_tm(s, h=h, linv_done=linv_done):
                                wait(s, "sve", linv_done)
                                last = None
                                for qc in range(4):
                                    last = s.activation(
                                        att_tm[:, 128 * qc:128 * qc + 128],
                                        bank(0, c0=128 * qc, c1=128 * qc + 128),
                                        AF.Copy,
                                        scale=linv[:, 4 * h + qc:4 * h + qc + 1])
                                inc(last, "sac")
                            blk.scalar(act_tm)
                            tm_done = cnt("sac")
                            exp_free[0] = tm_done

                            def pe_fm(t_, h=h, tm_done=tm_done, e1=exp_free[1]):
                                wait(t_, "sac", tm_done)
                                wait(t_, "sac", e1)
                                last = None
                                for qc in range(4):
                                    last = t_.matmul(
                                        bank(1, c0=128 * qc, c1=128 * qc + 128),
                                        att_tm[:, 128 * qc:128 * qc + 128],
                                        ident[:], start=True, stop=True)
                                inc(last, "spe")
                            blk.tensor(pe_fm)
                            fm_done = prev_pe = cnt("spe")

                            def act_fm(s, h=h, fm_done=fm_done):
                                wait(s, "spe", fm_done)
                                inc(s.copy(attf[:, QW * h:QW * h + QW], bank(1)),
                                    "sac")
                            blk.scalar(act_fm)
                            exp_free[1] = cnt("sac")
                    attf_done = cnt("sac")

                    if "attf" in dbg and t == 1:
                        def dmpa(s):
                            wait(s, "sac", attf_done)
                            inc(s.dma_start(out=dbg_outs["attf"], in_=attf[:]),
                                "sdbg", 16)
                            inc(s.dma_start(out=dbg_outs["lrow"], in_=l_sb[:]),
                                "sdbg", 16)
                        blk.gpsimd(dmpa)

                    for b4 in range(4):
                        def pe_wo(t_, t=t, b4=b4, attf_done=attf_done,
                                  wp=wo_prev, pev=cnt("sac"), mv=mtb_ve):
                            wait(t_, "sac", attf_done)
                            wait(t_, "sac", pev)
                            wait(t_, "sve", mv)
                            if b4 == 0:
                                wait(t_, "sac", wp)
                            last = None
                            for h in range(NH):
                                for n in range(4):
                                    last = t_.matmul(bank(4 + n),
                                                     attf[:, QW * h + 128 * b4:
                                                          QW * h + 128 * b4 + 128],
                                                     wo_sb[:, D * h + 512 * n:
                                                           D * h + 512 * n + 512],
                                                     start=(h == 0),
                                                     stop=(h == NH - 1))
                            inc(last, "spe")
                        blk.tensor(pe_wo)
                        wo_done = cnt("spe")

                        def act_wo(s, wo_done=wo_done, prev_st=cnt("srs")):
                            wait(s, "spe", wo_done)
                            wait(s, "srs", prev_st)
                            inc(s.copy(wo_out[:], pp[:, 2048:4096]), "sac")
                        blk.scalar(act_wo)
                        woe_done = cnt("sac")

                        def sy_wo(s, t=t, b4=b4, woe_done=woe_done):
                            wait(s, "sac", woe_done)
                            inc(s.dma_start(out=rs_in[512 * t + 128 * b4:
                                                      512 * t + 128 * b4 + 128, :],
                                            in_=wo_out[:]), "srs", 16)
                        blk.sync(sy_wo)
                    wo_prev = cnt("sac")
                    mtb_ac = wo_prev
                    qt_users[tb2] = cnt("spe")


                def g_rs(g):
                    wait(g, "srs", cnt("srs"))
                    inc(g.collective_compute(
                        "ReduceScatter", ALU.add,
                        replica_groups=[[0, 1, 2, 3], [4, 5, 6, 7]],
                        ins=[rs_in.ap()], outs=[rs_out.ap()]), "scc")
                blk.gpsimd(g_rs)

                if "woin" in dbg:
                    def dmpw(s):
                        wait(s, "srs", cnt("srs"))
                        inc(s.dma_start(out=dbg_outs["woin"], in_=rs_in.ap()),
                            "sdbg", 16)
                    blk.sync(dmpw)

            fence()
            ALO.close("A2")
            ALO.close("A")

            eB = ALO.enter_fn("B")
            xn_own = eB(nc.sbuf_tensor("xn_own", [128, NCI * TQ], BF16))
            gs_sb = eB(nc.sbuf_tensor("gs_sb", [128, NIC], F32))
            us_sb = eB(nc.sbuf_tensor("us_sb", [128, NIC], F32))
            ds_sb = eB(nc.sbuf_tensor("ds_sb", [128, NEC], F32))
            dn_fm = [eB(nc.sbuf_tensor(f"dn_fm{i}", [128, 1024], F32))
                     for i in range(2)]
            dn_ev = [eB(nc.sbuf_tensor(f"dn_ev{i}", [128, 128], F32))
                     for i in range(2)]
            gw_sb = [eB(nc.sbuf_tensor(f"gw_sb{i}", [128, D], FP8))
                     for i in range(2)]
            uw_sb = [eB(nc.sbuf_tensor(f"uw_sb{i}", [128, D], FP8))
                     for i in range(2)]
            gate_bf2 = [eB(nc.sbuf_tensor(f"gate_bf{i}", [128, 512], BF16))
                        for i in range(2)]
            eP4 = ALO.enter_fn("P4")
            xblk5 = [eP4(nc.sbuf_tensor(f"xblk5_{i}", [128, D], F32)) for i in range(4)]
            xn_tm5 = eP4(nc.sbuf_tensor("xn_tm5", [128, 4 * D], BF16))
            stat5 = eP4(nc.sbuf_tensor("stat5", [128, 20], F32))

            if True:
                def sy_sc(s):
                    wait(s, "sout", cnt("sout"))
                    wait(s, "sdbg", cnt("sdbg"))
                    wait(s, "sqt0", cnt("sqt0"))
                    wait(s, "sqt1", cnt("sqt1"))
                    inc(s.dma_start(out=gs_sb[:], in_=dgs), "sx", 16)
                    inc(s.dma_start(out=us_sb[:], in_=dus), "sx", 16)
                    inc(s.dma_start(out=ds_sb[:], in_=dds), "sx", 16)
                blk.sync(sy_sc)
                sc_ready = cnt("sx")

                def g_guard(g):
                    wait(g, "sout", cnt("sout"))
                    wait(g, "sdbg", cnt("sdbg"))
                    wait(g, "sqt0", cnt("sqt0"))
                    wait(g, "sqt1", cnt("sqt1"))
                blk.gpsimd(g_guard)

                def v_guard(v):
                    wait(v, "sout", cnt("sout"))
                    wait(v, "sdbg", cnt("sdbg"))
                    wait(v, "sqt0", cnt("sqt0"))
                    wait(v, "sqt1", cnt("sqt1"))
                blk.vector(v_guard)

                def a_guard(s):
                    wait(s, "sout", cnt("sout"))
                    wait(s, "sdbg", cnt("sdbg"))
                    wait(s, "sqt0", cnt("sqt0"))
                    wait(s, "sqt1", cnt("sqt1"))
                blk.scalar(a_guard)

                emit_ln_phase(blk, dxo, 4,
                              lambda ci, t: xn_own[:, TQ * ci + 512 * t:
                                                   TQ * ci + 512 * t + 512],
                              xblk5, xn_tm5, stat5, None)
                xn_own_done = cnt("sac")
            fence()
            ALO.close("P4")
            fuse_sb = eB(nc.sbuf_tensor("fuse_sb", [128, NIC * 512], BF16))
            dw_sb = [eB(nc.sbuf_tensor(f"dw_sb{i}", [128, NIC * 128], FP8))
                     for i in range(2)]
            fin = [eB(nc.sbuf_tensor(f"fin{i}", [128, D], F32)) for i in range(2)]

            if True:

                if "xnown" in dbg:
                    def dmpx(s):
                        wait(s, "sac", xn_own_done)
                        inc(s.dma_start(out=dbg_outs["xnown"], in_=xn_own[:]),
                            "sdbg", 16)
                    blk.gpsimd(dmpx)

                w_free = [0, 0]
                dwf = [0, 0]
                gu_bank_free = [[0, 0], [0, 0]]
                dn_bank_free = [0, 0]
                tr_bank_free = [0, 0]
                trctr = 0
                for ts in range(4):
                    for ic in range(NIC):
                        bi = ic % 2

                        def g_w(g, ic=ic, bi=bi, wf=w_free[bi],
                                pr=cnt("sprep")):
                            wait(g, "spe", wf)
                            wait(g, "sprep", pr)
                            inc(g.dma_start(out=gw_sb[bi][:, :],
                                            in_=gw_scr.ap()[128 * ic:
                                                            128 * ic + 128, :]),
                                f"sw5{bi}", 16)
                            inc(g.dma_start(out=uw_sb[bi][:, :],
                                            in_=uw_scr.ap()[128 * ic:
                                                            128 * ic + 128, :]),
                                f"sw5{bi}", 16)
                        blk.sync(g_w)
                        w_ready = cnt(f"sw5{bi}")

                        ip = ic % 2
                        gb = 4 + 2 * ip

                        def pe_gu(t_, ts=ts, bi=bi, gb=gb, w_ready=w_ready,
                                  xn_done=xn_own_done,
                                  prev_ev=gu_bank_free[ip][0],
                                  prev_fu=gu_bank_free[ip][1], scr=sc_ready):
                            wait(t_, f"sw5{bi}", w_ready)
                            wait(t_, "sx", scr)
                            wait(t_, "sac", xn_done)
                            wait(t_, "sac", prev_ev)
                            wait(t_, "sve", prev_fu)
                            last = None
                            for cc in range(NCI):
                                rhs = xn_own[:, TQ * cc + 512 * ts:
                                             TQ * cc + 512 * ts + 512]
                                t_.matmul(bank(gb),
                                          gw_sb[bi][:, 128 * cc:128 * cc + 128],
                                          rhs, start=(cc == 0),
                                          stop=(cc == NCI - 1))
                                last = t_.matmul(
                                    bank(gb + 1),
                                    uw_sb[bi][:, 128 * cc:128 * cc + 128],
                                    rhs, start=(cc == 0), stop=(cc == NCI - 1))
                            inc(last, "spe")
                        blk.tensor(pe_gu)
                        gu_done = cnt("spe")
                        w_free[bi] = gu_done

                        def act_gate(s, ic=ic, ip=ip, gb=gb, gu_done=gu_done):
                            wait(s, "spe", gu_done)
                            inc(s.activation(gate_bf2[ip][:], bank(gb), AF.Silu,
                                             scale=gs_sb[:, ic:ic + 1]), "sac")
                        blk.scalar(act_gate)
                        gate_done = cnt("sac")

                        def ve_fuse(v, ic=ic, ip=ip, gb=gb, gate_done=gate_done):
                            wait(v, "sac", gate_done)
                            inc(v.scalar_tensor_tensor(
                                fuse_sb[:, 512 * ic:512 * ic + 512],
                                bank(gb + 1), us_sb[:, ic:ic + 1],
                                gate_bf2[ip][:],
                                ALU.mult, ALU.mult), "sve")
                        blk.vector(ve_fuse)
                        gu_bank_free[ip] = [cnt("sac"), cnt("sve")]
                    fuse_done = cnt("sve")
                    dwf[0] = dwf[1] = max(dwf[0], dwf[1], cnt("spe"))

                    for ec in range(NEC):
                        bi = ec % 2

                        def g_dw(g, ec=ec, bi=bi, prev=dwf[bi],
                                 pr=cnt("sprep")):
                            wait(g, "spe", prev)
                            wait(g, "sprep", pr)
                            inc(g.dma_start(out=dw_sb[bi][:, :],
                                            in_=dw_scr.ap()[128 * ec:
                                                            128 * ec + 128, :]),
                                f"sw5{bi}", 16)
                        blk.sync(g_dw)
                        dw_ready = cnt(f"sw5{bi}")

                        ep = ec % 2
                        db = 2 * ep

                        def pe_dn(t_, bi=bi, ep=ep, db=db, dw_ready=dw_ready,
                                  fuse_done=fuse_done,
                                  prev_ev=dn_bank_free[ep]):
                            wait(t_, f"sw5{bi}", dw_ready)
                            wait(t_, "sve", fuse_done)
                            wait(t_, "sac", prev_ev)
                            last = None
                            for ic in range(NIC):
                                last = t_.matmul(
                                    bank(db),
                                    dw_sb[bi][:, 128 * ic:128 * ic + 128],
                                    fuse_sb[:, 512 * ic:512 * ic + 512],
                                    start=(ic == 0), stop=(ic == NIC - 1))
                            inc(last, "spe")
                        blk.tensor(pe_dn)
                        dn_done = cnt("spe")
                        dwf[bi] = dn_done

                        def act_dn(s, ec=ec, ep=ep, db=db, dn_done=dn_done):
                            wait(s, "spe", dn_done)
                            inc(s.activation(dn_fm[ep][:, 0:512], bank(db),
                                             AF.Copy,
                                             scale=ds_sb[:, ec:ec + 1]), "sac")
                        blk.scalar(act_dn)
                        dnf_done = cnt("sac")
                        dn_bank_free[ep] = dnf_done

                        for tb8 in range(4):
                            tb_ = 6 + trctr % 2
                            tslot = (trctr // 2) % 2
                            trctr += 1

                            def pe_tr5(t_, ep=ep, tb8=tb8, tb_=tb_, tslot=tslot,
                                       dnf_done=dnf_done,
                                       prev_ev=tr_bank_free[tb_ - 6]):
                                wait(t_, "sac", dnf_done)
                                wait(t_, "sac", prev_ev)
                                inc(t_.matmul(bank(tb_, c0=128 * tslot,
                                                  c1=128 * tslot + 128),
                                              dn_fm[ep][:, 128 * tb8:
                                                        128 * tb8 + 128],
                                              identf[:], start=True, stop=True),
                                    "spe")
                            blk.tensor(pe_tr5)
                            tr_done = cnt("spe")

                            def act_tr5(s, tb_=tb_, tslot=tslot, tb8=tb8,
                                        tr_done=tr_done, prev_st=cnt("srs")):
                                wait(s, "spe", tr_done)
                                wait(s, "srs", prev_st)
                                inc(s.copy(dn_ev[tb8 % 2][:],
                                           bank(tb_, c0=128 * tslot,
                                                c1=128 * tslot + 128)), "sac")
                            blk.scalar(act_tr5)
                            ev_done = cnt("sac")
                            tr_bank_free[tb_ - 6] = ev_done

                            def sy_tr5(s, ts=ts, ec=ec, tb8=tb8, ev_done=ev_done):
                                wait(s, "sac", ev_done)
                                inc(s.dma_start(
                                    out=dn_scr.ap()[512 * ts + 128 * tb8:
                                                    512 * ts + 128 * tb8 + 128,
                                                    128 * ec:128 * ec + 128],
                                    in_=dn_ev[tb8 % 2][:]), "srs", 16)
                            blk.sync(sy_tr5)
                    w_free[0] = w_free[1] = cnt("spe")
                    gu_bank_free[0] = [cnt("sac"), gu_bank_free[0][1]]
                    gu_bank_free[1] = [cnt("sac"), gu_bank_free[1][1]]

                dn_all = cnt("srs")
                for tb in range(TQ // 128):
                    def sy_fin(s, tb=tb, dn_all=dn_all, prev=cnt("sve"),
                               prev_out=cnt("sout")):
                        wait(s, "srs", dn_all)
                        wait(s, "scc", 1)
                        wait(s, "sve", prev)
                        wait(s, "sout", prev_out)
                        inc(s.dma_start(out=fin[0][:],
                                        in_=dxo[128 * tb:128 * tb + 128, :]), "sx", 16)
                        inc(s.dma_start(out=fin[1][:],
                                        in_=rs_out.ap()[128 * tb:128 * tb + 128, :]),
                            "sx", 16)
                    blk.sync(sy_fin)
                    f_ready = cnt("sx")

                    def ve_fin(v, f_ready=f_ready):
                        wait(v, "sx", f_ready)
                        inc(v.tensor_add(fin[0][:], fin[0][:], fin[1][:]), "sve")
                    blk.vector(ve_fin)
                    add1 = cnt("sve")

                    def sy_fin2(s, tb=tb, add1=add1):
                        wait(s, "sve", add1)
                        inc(s.dma_start(out=fin[1][:],
                                        in_=dn_scr.ap()[128 * tb:128 * tb + 128, :]),
                            "sx", 16)
                    blk.sync(sy_fin2)
                    f2_ready = cnt("sx")

                    def ve_fin2(v, f2_ready=f2_ready):
                        wait(v, "sx", f2_ready)
                        inc(v.tensor_add(fin[0][:], fin[0][:], fin[1][:]), "sve")
                    blk.vector(ve_fin2)
                    fin_done = cnt("sve")

                    def sy_out(s, tb=tb, fin_done=fin_done):
                        wait(s, "sve", fin_done)
                        inc(s.dma_start(out=dout[128 * tb:128 * tb + 128, :],
                                        in_=fin[0][:]), "sout", 16)
                    blk.sync(sy_out)

                def g_end(g):
                    wait(g, "sout", cnt("sout"))
                    wait(g, "sdbg", cnt("sdbg"))
                blk.gpsimd(g_end)

            ALO.close("B")

            _chk = {n: v[1] for n, v in sems.items()}
            if _finals is None:
                _finals = _chk
            else:
                assert _finals == _chk, (_live, _finals, _chk)
        _real_blk.__exit__(None, None, None)


    return nc, dbg_outs



def _consts():
    inv = (10000.0 ** (-np.arange(64, dtype=np.float64) / 64.0))
    ang = np.arange(T, dtype=np.float64)[None, :] * inv[:, None]
    cos64 = np.cos(ang) * RSC
    sin64 = np.sin(ang) * RSC
    cos_t = np.concatenate([cos64, cos64], 0).astype(np.float32)
    sin_t = np.concatenate([-sin64, sin64], 0).astype(np.float32)
    ident = np.eye(128, dtype=np.float32)
    jio = (np.arange(NKC, dtype=np.float32)[None, :] * 128.0
           + np.arange(128, dtype=np.float32)[:, None])
    qio = np.arange(QW, dtype=np.float32)[None, :]
    mask = np.zeros((4 * 128, QW), dtype=np.float32)
    j = np.arange(128)[:, None]
    q = np.arange(QW)[None, :]
    for r in range(4):
        mask[128 * r:128 * r + 128] = np.where(q >= j + 128 * r, 0.0, NEG)
    return cos_t, sin_t, ident, jio, qio, mask


def make_in_maps(inputs):
    x = np.ascontiguousarray(np.asarray(inputs["L0_attn_x"], dtype=np.float32))
    wq = np.asarray(inputs["L0_attn_Wq"], dtype=np.float32)
    wk = np.asarray(inputs["L0_attn_Wk"], dtype=np.float32)
    wv = np.asarray(inputs["L0_attn_Wv"], dtype=np.float32)
    wo = np.asarray(inputs["L0_attn_Wo"], dtype=np.float32)
    wa = np.ascontiguousarray(np.asarray(inputs["L0_attn_pol_WA"], dtype=np.float32))
    wb = np.ascontiguousarray(np.asarray(inputs["L0_attn_pol_WB"], dtype=np.float32))
    pol_dir = np.asarray(inputs["L0_attn_pol_dir"], dtype=np.float32)
    pol_gate = np.asarray(inputs["L0_attn_pol_gate"], dtype=np.float32)
    gtp = np.asarray(inputs["L0_attn_gtp_gamma"], dtype=np.float32)
    gw = np.ascontiguousarray(np.asarray(inputs["L0_ffn_gate_w"], dtype=np.float32))
    uw = np.ascontiguousarray(np.asarray(inputs["L0_ffn_up_w"], dtype=np.float32))
    dwn = np.ascontiguousarray(np.asarray(inputs["L0_ffn_down_w"], dtype=np.float32))
    gs = np.asarray(inputs["L0_ffn_gate_scale"], dtype=np.float32)
    us = np.asarray(inputs["L0_ffn_up_scale"], dtype=np.float32)
    ds = np.asarray(inputs["L0_ffn_down_scale"], dtype=np.float32)

    cos_t, sin_t, ident, jio, qio, mask = _consts()
    gs_r = np.ascontiguousarray(gs.reshape(NIC, 128).T)
    us_r = np.ascontiguousarray(us.reshape(NIC, 128).T)
    ds_r = np.ascontiguousarray(ds.reshape(NEC, 128).T)

    in_maps = []
    for c in range(8):
        b, g = c // 4, c % 4
        pol3 = np.concatenate([pol_dir[4 * g:4 * g + 4],
                               pol_gate[4 * g:4 * g + 4],
                               gtp[4 * g:4 * g + 4]]).astype(np.float32).reshape(1, 12)
        in_maps.append({
            "x_full": x[b],
            "x_own": np.ascontiguousarray(x[b, TQ * g:TQ * g + TQ]),
            "wq": np.ascontiguousarray(wq[:, 512 * g:512 * g + 512]),
            "wk": np.ascontiguousarray(wk[:, 128 * g:128 * g + 128]),
            "wv": np.ascontiguousarray(wv[:, 128 * g:128 * g + 128]),
            "wa": wa, "wb": wb,
            "wo_my": np.ascontiguousarray(wo[512 * g:512 * g + 512, :]),
            "gate_w": gw, "up_w": uw, "down_w": dwn,
            "gs_r": gs_r, "us_r": us_r, "ds_r": ds_r,
            "pol3": pol3,
            "cos_t": cos_t, "sin_t": sin_t, "ident": ident,
            "jiota": jio, "qiota": qio, "maskneg": mask,
        })
    return in_maps


def kernel(**inputs):
    in_maps = make_in_maps(inputs)
    nc, _ = build()
    res = run_bass_kernel_spmd(nc, in_maps, core_ids=list(range(8)))
    out = np.empty((2, T, D), dtype=np.float32)
    for c in range(8):
        b, g = c // 4, c % 4
        out[b, TQ * g:TQ * g + TQ] = res.results[c]["out"]
    return out


# version 12 (final)
# speedup vs baseline: 1.1274x; 1.1274x over previous
import numpy as np
import ml_dtypes
from contextlib import ExitStack

import concourse.bass as bass
import concourse.mybir as mybir
from concourse.bass_utils import run_bass_kernel_spmd

F32 = mybir.dt.float32
BF16 = mybir.dt.bfloat16
FP8 = mybir.dt.float8e4
AF = mybir.ActivationFunctionType
ALU = mybir.AluOpType
AX = mybir.AxisListType

T = 8192
D = 2048
NH = 4
DH = 128
RANK = 32
II = 6144
TQ = 2048
NT = 16
QW = 512
NKC = 64
NCI = 16
NIC = II // 128
NEC = D // 128
EPS = 1e-5
RSC = 1.0 / (128.0 ** 0.25)
MARGIN = 8.0
NEG = -1e30


class _DummyInstr:
    def then_inc(self, *a, **k):
        return self

    def annotate(self, *a, **k):
        return self


class _DummyEngine:
    def __getattr__(self, name):
        def f(*a, **k):
            return _DummyInstr()
        return f


_DUMMY = _DummyEngine()


class _EM:

    def __init__(self, blk, live):
        self._blk = blk
        self._live = live

    def _d(self, name, f):
        if name == self._live:
            getattr(self._blk, name)(f)
        else:
            f(_DUMMY)

    def sync(self, f):
        self._d("sync", f)

    def gpsimd(self, f):
        self._d("gpsimd", f)

    def vector(self, f):
        self._d("vector", f)

    def scalar(self, f):
        self._d("scalar", f)

    def tensor(self, f):
        self._d("tensor", f)


class _Alloc:
    def __init__(self):
        self.stacks = {}
        self.cache = {}
        self.ctr = {}
        self.first = True

    def begin_pass(self, first):
        self.first = first
        self.ctr = {}

    def enter_fn(self, sname):
        if self.first and sname not in self.stacks:
            self.stacks[sname] = ExitStack()

        def enter(cm, sname=sname):
            i = self.ctr.get(sname, 0)
            self.ctr[sname] = i + 1
            k = (sname, i)
            if self.first:
                self.cache[k] = self.stacks[sname].enter_context(cm)
            return self.cache[k]
        return enter

    def close(self, sname):
        if self.first:
            self.stacks[sname].close()



def build(dbg=()):
    nc = bass.Bass("TRN2", target_bir_lowering=False)

    dx = nc.dram_tensor("x_full", [T, D], F32, kind="ExternalInput").ap()
    dxo = nc.dram_tensor("x_own", [TQ, D], F32, kind="ExternalInput").ap()
    dwq = nc.dram_tensor("wq", [D, NH * DH], F32, kind="ExternalInput").ap()
    dwk = nc.dram_tensor("wk", [D, DH], F32, kind="ExternalInput").ap()
    dwv = nc.dram_tensor("wv", [D, DH], F32, kind="ExternalInput").ap()
    dwa = nc.dram_tensor("wa", [D, RANK], F32, kind="ExternalInput").ap()
    dwb = nc.dram_tensor("wb", [D, RANK], F32, kind="ExternalInput").ap()
    dwo = nc.dram_tensor("wo_my", [NH * DH, D], F32, kind="ExternalInput").ap()
    dgw = nc.dram_tensor("gate_w", [D, II], F32, kind="ExternalInput").ap()
    duw = nc.dram_tensor("up_w", [D, II], F32, kind="ExternalInput").ap()
    ddw = nc.dram_tensor("down_w", [II, D], F32, kind="ExternalInput").ap()
    dgs = nc.dram_tensor("gs_r", [128, NIC], F32, kind="ExternalInput").ap()
    dus = nc.dram_tensor("us_r", [128, NIC], F32, kind="ExternalInput").ap()
    dds = nc.dram_tensor("ds_r", [128, NEC], F32, kind="ExternalInput").ap()
    dpol = nc.dram_tensor("pol3", [1, 12], F32, kind="ExternalInput").ap()
    dcos = nc.dram_tensor("cos_t", [128, T], F32, kind="ExternalInput").ap()
    dsin = nc.dram_tensor("sin_t", [128, T], F32, kind="ExternalInput").ap()
    dident = nc.dram_tensor("ident", [128, 128], F32, kind="ExternalInput").ap()
    djio = nc.dram_tensor("jiota", [128, NKC], F32, kind="ExternalInput").ap()
    dqio = nc.dram_tensor("qiota", [1, QW], F32, kind="ExternalInput").ap()
    dmask = nc.dram_tensor("maskneg", [4 * 128, QW], F32, kind="ExternalInput").ap()
    dout = nc.dram_tensor("out", [TQ, D], F32, kind="ExternalOutput").ap()

    q_scr = nc.dram_tensor("q_scr", [128, NH * T], BF16)
    gw_scr = nc.dram_tensor("gw_scr", [NIC * 128, D], FP8)
    uw_scr = nc.dram_tensor("uw_scr", [NIC * 128, D], FP8)
    dw_scr = nc.dram_tensor("dw_scr", [NEC * 128, II], FP8)
    rs_in = nc.dram_tensor("rs_in", [T, D], F32)
    rs_out = nc.dram_tensor("rs_out", [TQ, D], F32)
    dn_scr = nc.dram_tensor("dn_scr", [TQ, D], F32)

    dbg_outs = {}
    for name, shape in (("xnfm", [128, NCI * 512]), ("q", [128, NH * T]),
                        ("k", [128, T]), ("v", [128, T]), ("a", [32, T]),
                        ("b", [32, T]), ("attf", [128, NH * QW]),
                        ("lrow", [128, QW]), ("woin", [T, D]),
                        ("xnown", [128, NCI * TQ]), ("fuse", [128, NIC * 1024])):
        if name in dbg:
            dbg_outs[name] = nc.dram_tensor("dbg_" + name, shape, F32,
                                            kind="ExternalOutput").ap()

    ctx = ExitStack()
    with ctx:
        e = ctx.enter_context

        pp = e(nc.psum_tensor("pp", [128, 4096], F32))

        def bank(i, p0=0, p1=128, c0=0, c1=512):
            return pp[p0:p1, 512 * i + c0:512 * i + c1]

        ident = e(nc.sbuf_tensor("ident_sb", [128, 128], BF16))
        identf = e(nc.sbuf_tensor("identf", [128, 128], F32))
        ones_bf = e(nc.sbuf_tensor("ones_bf", [128, 1], BF16))
        onesf = e(nc.sbuf_tensor("onesf", [128, 1], F32))
        onesrow_bf = e(nc.sbuf_tensor("onesrow_bf", [1, 128], BF16))
        onesrow_f = e(nc.sbuf_tensor("onesrow_f", [1, 128], F32))
        prm = e(nc.sbuf_tensor("prm", [1, 32], F32))
        t11 = e(nc.sbuf_tensor("t11", [1, 16], F32))
        cvec = e(nc.sbuf_tensor("cvec", [128, NH], F32))
        ghvec = e(nc.sbuf_tensor("ghvec", [128, NH], F32))

        sems = {}
        for name in ("sx", "sgd", "spe", "sve", "sac", "scc", "sout",
                     "srs", "sdbg", "sxb0", "sxb1", "sxb2", "sxb3", "scs0", "scs1",
                     "ssw0", "ssw1", "sqo0", "sqo1", "sqt0", "sqt1",
                     "sw50", "sw51", "sdw0", "sdw1", "sprep"):
            sems[name] = [e(nc.semaphore(name)), 0]

        def inc(instr, name, k=1):
            s = sems[name]
            instr.then_inc(s[0], k)
            s[1] += k
            return s[1]

        def cnt(name):
            return sems[name][1]

        def wait(eng, name, val):
            if val > 0:
                eng.wait_ge(sems[name][0], val)

        ALO = _Alloc()
        _finals = None
        _real_blk = nc.Block()
        _real_blk.__enter__()
        for _pi, _live in enumerate(("sync", "gpsimd", "vector", "scalar",
                                     "tensor")):
            ALO.begin_pass(_pi == 0)
            for _s in sems.values():
                _s[1] = 0
            blk = _EM(_real_blk, _live)

            def fence(exclude=("scc",)):
                snap = [(n, sems[n][1]) for n in sems if n not in exclude]

                def fw(eng, snap=snap):
                    for n, v in snap:
                        wait(eng, n, v)
                for _en in ("sync", "gpsimd", "vector", "scalar", "tensor"):
                    blk._d(_en, fw)

            eA = ALO.enter_fn("A")
            k_sb = eA(nc.sbuf_tensor("k_sb", [128, T], BF16))
            v_tm = eA(nc.sbuf_tensor("v_tm", [128, T], BF16))
            a_sb = eA(nc.sbuf_tensor("a_sb", [64, T], BF16))
            b_sb = eA(nc.sbuf_tensor("b_sb", [64, T], BF16))
            wq_sb = eA(nc.sbuf_tensor("wq_sb", [128, NCI * 512], BF16))
            wk_sb = eA(nc.sbuf_tensor("wk_sb", [128, NCI * 128], BF16))
            wv_sb = eA(nc.sbuf_tensor("wv_sb", [128, NCI * 128], BF16))
            wa_sb = eA(nc.sbuf_tensor("wa_sb", [128, NCI * 32], BF16))
            wb_sb = eA(nc.sbuf_tensor("wb_sb", [128, NCI * 32], BF16))

            if True:
                def g_loads(g):
                    inc(g.dma_start(out=ident[:], in_=dident), "sgd", 16)
                    inc(g.dma_start(out=identf[:], in_=dident), "sgd", 16)
                    for ci in range(NCI):
                        inc(g.dma_start(out=wq_sb[:, 512 * ci:512 * ci + 512],
                                        in_=dwq[128 * ci:128 * ci + 128, :]), "sgd", 16)
                        inc(g.dma_start(out=wk_sb[:, 128 * ci:128 * ci + 128],
                                        in_=dwk[128 * ci:128 * ci + 128, :]), "sgd", 16)
                        inc(g.dma_start(out=wv_sb[:, 128 * ci:128 * ci + 128],
                                        in_=dwv[128 * ci:128 * ci + 128, :]), "sgd", 16)
                        inc(g.dma_start(out=wa_sb[:, 32 * ci:32 * ci + 32],
                                        in_=dwa[128 * ci:128 * ci + 128, :]), "sgd", 16)
                        inc(g.dma_start(out=wb_sb[:, 32 * ci:32 * ci + 32],
                                        in_=dwb[128 * ci:128 * ci + 128, :]), "sgd", 16)
                blk.gpsimd(g_loads)

                def s_loads(s):
                    inc(s.dma_start(out=prm[0:1, 0:12], in_=dpol), "sx", 16)
                blk.sync(s_loads)

                def v_init(v):
                    wait(v, "sgd", cnt("sgd"))
                    wait(v, "sx", cnt("sx"))
                    v.memset(ones_bf[:], 1.0)
                    v.memset(onesf[:], 1.0)
                    v.memset(onesrow_bf[:], 1.0)
                    v.memset(onesrow_f[:], 1.0)
                    inc(v.tensor_scalar_min(t11[0:1, 0:4], prm[0:1, 0:4], 1.0), "sve")
                    wait(v, "sve", cnt("sve"))
                    inc(v.tensor_scalar_max(t11[0:1, 0:4], t11[0:1, 0:4], -1.0), "sve")
                    wait(v, "sve", cnt("sve"))
                    v.tensor_scalar_mul(t11[0:1, 0:4], t11[0:1, 0:4], 1.0 / 4096.0)
                    inc(v.tensor_scalar_mul(t11[0:1, 12:16], prm[0:1, 8:12], -1.0),
                        "sve")
                    wait(v, "sve", cnt("sve"))
                    inc(v.tensor_max(t11[0:1, 12:16], t11[0:1, 12:16],
                                     prm[0:1, 8:12]), "sve")
                blk.vector(v_init)

                def a_init(s):
                    wait(s, "sve", cnt("sve"))
                    inc(s.activation(t11[0:1, 4:8], t11[0:1, 12:16], AF.Exp,
                                     scale=-1.0), "sac")
                    wait(s, "sac", cnt("sac"))
                    inc(s.activation(t11[0:1, 8:12], t11[0:1, 4:8], AF.Ln,
                                     bias=1.0), "sac")
                    wait(s, "sac", cnt("sac"))
                    inc(s.activation(t11[0:1, 4:8], prm[0:1, 4:8], AF.Tanh,
                                     scale=0.5), "sac")
                blk.scalar(a_init)

                def v_init2(v):
                    wait(v, "sac", cnt("sac"))
                    inc(v.tensor_scalar_max(t11[0:1, 12:16], prm[0:1, 8:12], 0.0),
                        "sve")
                    wait(v, "sve", cnt("sve"))
                    inc(v.tensor_add(t11[0:1, 8:12], t11[0:1, 8:12],
                                     t11[0:1, 12:16]), "sve")
                    wait(v, "sve", cnt("sve"))
                    inc(v.tensor_scalar_max(t11[0:1, 8:12], t11[0:1, 8:12], 1e-6),
                        "sve")
                    wait(v, "sve", cnt("sve"))
                    inc(v.tensor_add(prm[0:1, 12:16], t11[0:1, 0:4], t11[0:1, 8:12]),
                        "sve")
                    wait(v, "sve", cnt("sve"))
                    v.tensor_scalar(prm[0:1, 16:20], t11[0:1, 4:8], 0.25, 0.25,
                                    ALU.mult, ALU.add)
                    inc(v.tensor_scalar_mul(prm[0:1, 20:24], prm[0:1, 12:16], -1.0),
                        "sve")
                blk.vector(v_init2)

                def t_bcast(t):
                    wait(t, "sve", cnt("sve"))
                    last = None
                    for h in range(NH):
                        t.matmul(bank(0, c0=h, c1=h + 1), onesrow_f[:],
                                 prm[0:1, 12 + h:13 + h], start=True, stop=True)
                        last = t.matmul(bank(0, c0=8 + h, c1=9 + h), onesrow_f[:],
                                        prm[0:1, 16 + h:17 + h], start=True, stop=True)
                    inc(last, "spe")
                blk.tensor(t_bcast)

                def a_bcast(s):
                    wait(s, "spe", cnt("spe"))
                    s.copy(cvec[:], bank(0, c0=0, c1=4))
                    inc(s.copy(ghvec[:], bank(0, c0=8, c1=12)), "sac")
                blk.scalar(a_bcast)

            fence()

            def emit_ln_phase(blk, src_ap, n_tiles, fm_dst_fn, xblk, xn_tm, stat,
                              rope_cfg):
                xfree = [[0, 0] for _ in range(4)]
                prev_stats = [0]
                tile_tail_sac = 0
                tr_prev = 0

                for t in range(n_tiles):
                    for b in range(4):
                        gi = 4 * t + b

                        def dma_x(s, gi=gi, i=b):
                            wait(s, "sac", xfree[i][0])
                            wait(s, "sve", xfree[i][1])
                            inc(s.dma_start(out=xblk[i][:],
                                            in_=src_ap[128 * gi:128 * gi + 128, :]),
                                f"sxb{i}", 16)
                        blk.sync(dma_x)

                    if rope_cfg is not None:
                        cs = rope_cfg
                        slot = t % 2

                        def dma_cs(s, t=t, slot=slot):
                            wait(s, "sve", cs["cs_free"][slot])
                            inc(s.dma_start(out=cs["cos_sb"][:, 512 * slot:512 * slot + 512],
                                            in_=dcos[:, 512 * t:512 * t + 512]),
                                f"scs{slot}", 16)
                            inc(s.dma_start(out=cs["sin_sb"][:, 512 * slot:512 * slot + 512],
                                            in_=dsin[:, 512 * t:512 * t + 512]),
                                f"scs{slot}", 16)
                        blk.sync(dma_cs)
                        cs["cs_ready"][slot] = cnt(f"scs{slot}")

                    sq_counts = []
                    red_counts = []
                    for b in range(4):
                        gi = 4 * t + b
                        i = b
                        xneed = cnt(f"sxb{i}")

                        def act_sq(s, i=i, b=b, xneed=xneed,
                                   pv=prev_stats[0], tr_prev=tr_prev):
                            wait(s, f"sxb{i}", xneed)
                            wait(s, "sve", pv)
                            wait(s, "spe", tr_prev)
                            inc(s.activation(xn_tm[:, D * b:D * b + D], xblk[i][:],
                                             AF.Square,
                                             accum_out=stat[:, 4 + b:5 + b]), "sac")
                        blk.scalar(act_sq)
                        sq_counts.append(cnt("sac"))

                        def ve_red(v, i=i, b=b, xneed=xneed, pv=prev_stats[0]):
                            wait(v, f"sxb{i}", xneed)
                            wait(v, "sve", pv)
                            inc(v.reduce_sum(stat[:, b:b + 1], xblk[i][:],
                                             axis=AX.X), "sve")
                        blk.vector(ve_red)
                        red_counts.append(cnt("sve"))

                    def ve_chain(v, sq_all=sq_counts[-1], red_all=red_counts[-1]):
                        wait(v, "sac", sq_all)
                        wait(v, "sve", red_all)
                        inc(v.tensor_scalar_mul(stat[:, 8:12], stat[:, 0:4],
                                                1.0 / D), "sve")
                        wait(v, "sve", cnt("sve"))
                        inc(v.tensor_mul(stat[:, 12:16], stat[:, 8:12],
                                         stat[:, 8:12]), "sve")
                        wait(v, "sve", cnt("sve"))
                        inc(v.scalar_tensor_tensor(stat[:, 4:8], stat[:, 4:8],
                                                   1.0 / D, stat[:, 12:16],
                                                   ALU.mult, ALU.subtract), "sve")
                        wait(v, "sve", cnt("sve"))
                        inc(v.tensor_scalar_add(stat[:, 4:8], stat[:, 4:8], EPS),
                            "sve")
                        wait(v, "sve", cnt("sve"))
                        inc(v.reciprocal(stat[:, 12:16], stat[:, 4:8]), "sve")
                    blk.vector(ve_chain)
                    rsq_done = cnt("sve")

                    def act_sqrt(s, rsq_done=rsq_done):
                        wait(s, "sve", rsq_done)
                        inc(s.activation(stat[:, 16:20], stat[:, 12:16], AF.Sqrt),
                            "sac")
                    blk.scalar(act_sqrt)
                    rstd_done = cnt("sac")

                    def ve_mr(v, rstd_done=rstd_done):
                        wait(v, "sac", rstd_done)
                        inc(v.scalar_tensor_tensor(stat[:, 0:4], stat[:, 8:12], -1.0,
                                                   stat[:, 16:20], ALU.mult,
                                                   ALU.mult), "sve")
                    blk.vector(ve_mr)
                    mr_done = cnt("sve")

                    for b in range(4):
                        i = b

                        def ve_xn(v, i=i, b=b, mr_done=mr_done):
                            wait(v, "sve", mr_done)
                            inc(v.tensor_scalar(xn_tm[:, D * b:D * b + D], xblk[i][:],
                                                stat[:, 16 + b:17 + b],
                                                stat[:, b:b + 1],
                                                ALU.mult, ALU.add), "sve")
                        blk.vector(ve_xn)
                        xfree[i] = [sq_counts[b], cnt("sve")]
                    prev_stats[0] = cnt("sve")
                    xn_ready = cnt("sve")

                    evac_done = {}
                    for ci in range(NCI):
                        tb = 6 + (ci % 2)
                        prev = evac_done.get(ci - 2, tile_tail_sac)

                        def pe_tr(t_, ci=ci, tb=tb, prev=prev, xn_ready=xn_ready):
                            wait(t_, "sve", xn_ready)
                            wait(t_, "sac", prev)
                            last = None
                            for b in range(4):
                                last = t_.matmul(
                                    bank(tb, c0=128 * b, c1=128 * b + 128),
                                    xn_tm[:, D * b + 128 * ci:D * b + 128 * ci + 128],
                                    ident[:], start=True, stop=True)
                            inc(last, "spe")
                        blk.tensor(pe_tr)
                        tr_done = cnt("spe")
                        if ci == NCI - 1:
                            tr_prev = tr_done

                        def act_ev(s, ci=ci, tb=tb, tr_done=tr_done, t=t):
                            wait(s, "spe", tr_done)
                            inc(s.copy(fm_dst_fn(ci, t), bank(tb)), "sac")
                        blk.scalar(act_ev)
                        evac_done[ci] = cnt("sac")

                    if rope_cfg is None:
                        tile_tail_sac = cnt("sac")
                        continue

                    cs = rope_cfg
                    all_evac = cnt("sac")
                    xn_fm = cs["xn_fm"]

                    def pe_proj(t_, t=t, all_evac=all_evac,
                                pb=cs["proj_bank_free"][0]):
                        wait(t_, "sac", all_evac)
                        wait(t_, "sac", pb)
                        for ci in range(NCI):
                            st = ci == 0
                            sp = ci == NCI - 1
                            rhs = xn_fm[:, 512 * ci:512 * ci + 512]
                            for qh in range(4):
                                t_.matmul(bank(qh),
                                          wq_sb[:, 512 * ci + 128 * qh:
                                                512 * ci + 128 * qh + 128],
                                          rhs, start=st, stop=sp)
                            t_.matmul(bank(4), wk_sb[:, 128 * ci:128 * ci + 128], rhs,
                                      start=st, stop=sp)
                            t_.matmul(bank(5), wv_sb[:, 128 * ci:128 * ci + 128], rhs,
                                      start=st, stop=sp)
                            t_.matmul(bank(6, p0=32, p1=64),
                                      wa_sb[:, 32 * ci:32 * ci + 32], rhs,
                                      start=st, stop=sp)
                            last = t_.matmul(bank(7, p0=32, p1=64),
                                             wb_sb[:, 32 * ci:32 * ci + 32], rhs,
                                             start=st, stop=sp)
                        inc(last, "spe")
                    blk.tensor(pe_proj)
                    proj_done = cnt("spe")

                    slot = t % 2
                    qst = cs["qstage"][slot]
                    for j in range(5):
                        pair = j % 2
                        qf = cs["qf32"][pair]
                        qs2 = cs["qswap"][pair]

                        def act_evq(s, j=j, qf=qf, proj_done=proj_done,
                                    pv=cs["pair_free"][pair]):
                            wait(s, "spe", proj_done)
                            wait(s, "sve", pv)
                            inc(s.copy(qf[:], bank(j)), "sac")
                        blk.scalar(act_evq)
                        ev_done = cnt("sac")

                        def g_swap(g, qf=qf, qs2=qs2, ev_done=ev_done, pair=pair):
                            wait(g, "sac", ev_done)
                            inc(g.dma_start(out=qs2[0:64, :], in_=qf[64:128, :]),
                                f"ssw{pair}", 16)
                            inc(g.dma_start(out=qs2[64:128, :], in_=qf[0:64, :]),
                                f"ssw{pair}", 16)
                        blk.gpsimd(g_swap)
                        sw_done = cnt(f"ssw{pair}")

                        dst = (qst[:, 512 * j:512 * j + 512] if j < 4
                               else k_sb[:, 512 * t:512 * t + 512])

                        def ve_rope(v, qf=qf, qs2=qs2, sw_done=sw_done, dst=dst,
                                    slot=slot, qw=cs["qst_free"][slot], pair=pair):
                            wait(v, f"ssw{pair}", sw_done)
                            wait(v, f"scs{slot}", cs["cs_ready"][slot])
                            wait(v, f"sqo{slot}", qw)
                            v.tensor_mul(qs2[:], qs2[:],
                                         cs["sin_sb"][:, 512 * slot:512 * slot + 512])
                            inc(v.tensor_mul(qf[:], qf[:],
                                             cs["cos_sb"][:, 512 * slot:512 * slot + 512]),
                                "sve")
                            wait(v, "sve", cnt("sve"))
                            inc(v.tensor_add(dst, qf[:], qs2[:]), "sve")
                        blk.vector(ve_rope)
                        cs["pair_free"][pair] = cnt("sve")
                    cs["cs_free"][slot] = cnt("sve")
                    rope_all = cnt("sve")

                    def g_qout(g, t=t, qst=qst, rope_all=rope_all, slot=slot):
                        wait(g, "sve", rope_all)
                        for j in range(4):
                            inc(g.dma_start(out=q_scr.ap()[:, T * j + 512 * t:
                                                           T * j + 512 * t + 512],
                                            in_=qst[:, 512 * j:512 * j + 512]),
                                f"sqo{slot}", 16)
                    blk.gpsimd(g_qout)
                    cs["qst_free"][slot] = cnt(f"sqo{slot}")

                    def act_evv(s, proj_done=proj_done):
                        wait(s, "spe", proj_done)
                        inc(s.copy(cs["vtmp"][:], bank(5)), "sac")
                    blk.scalar(act_evv)
                    vtmp_done = cnt("sac")

                    def pe_vtr(t_, vtmp_done=vtmp_done):
                        wait(t_, "sac", vtmp_done)
                        last = None
                        for jj in range(4):
                            last = t_.matmul(bank(5, c0=128 * jj, c1=128 * jj + 128),
                                             cs["vtmp"][:, 128 * jj:128 * jj + 128],
                                             ident[:], start=True, stop=True)
                        inc(last, "spe")
                    blk.tensor(pe_vtr)
                    vtr_done = cnt("spe")

                    def act_evv2(s, t=t, vtr_done=vtr_done):
                        wait(s, "spe", vtr_done)
                        s.copy(v_tm[:, 512 * t:512 * t + 512], bank(5))
                        s.copy(a_sb[32:64, 512 * t:512 * t + 512],
                               bank(6, p0=32, p1=64))
                        inc(s.copy(b_sb[32:64, 512 * t:512 * t + 512],
                                   bank(7, p0=32, p1=64)), "sac")
                    blk.scalar(act_evv2)
                    tile_tail_sac = cnt("sac")
                    cs["proj_bank_free"][0] = cnt("sac")

            e1 = ALO.enter_fn("c1")
            xblk = [e1(nc.sbuf_tensor(f"xblk{i}", [128, D], F32)) for i in range(4)]
            xn_tm = e1(nc.sbuf_tensor("xn_tm", [128, 4 * D], BF16))
            xn_fm = e1(nc.sbuf_tensor("xn_fm", [128, NCI * 512], BF16))
            stat = e1(nc.sbuf_tensor("stat", [128, 20], F32))
            cos_sb = e1(nc.sbuf_tensor("cos_sb", [128, 1024], F32))
            sin_sb = e1(nc.sbuf_tensor("sin_sb", [128, 1024], F32))
            qf32 = [e1(nc.sbuf_tensor(f"qf32_{i}", [128, 512], F32)) for i in range(2)]
            qswap = [e1(nc.sbuf_tensor(f"qswap{i}", [128, 512], F32)) for i in range(2)]
            vtmp = e1(nc.sbuf_tensor("vtmp", [128, 512], BF16))
            qstage = [e1(nc.sbuf_tensor(f"qstage{i}", [128, 4 * 512], BF16))
                      for i in range(2)]

            rope_cfg = {"cos_sb": cos_sb, "sin_sb": sin_sb, "qf32": qf32,
                        "qswap": qswap, "vtmp": vtmp, "xn_fm": xn_fm,
                        "qstage": qstage, "qst_free": [0, 0],
                        "cs_ready": [0, 0], "cs_free": [0, 0],
                        "pair_free": [0, 0], "proj_bank_free": [0]}

            if True:
                emit_ln_phase(blk, dx, NT, lambda ci, t: xn_fm[:, 512 * ci:512 * ci + 512],
                              xblk, xn_tm, stat, rope_cfg)
                if "xnfm" in dbg:
                    def dmp(s):
                        wait(s, "sac", cnt("sac"))
                        inc(s.dma_start(out=dbg_outs["xnfm"], in_=xn_fm[:]), "sdbg", 16)
                    blk.gpsimd(dmp)

            ALO.close("c1")
            fence()

            eA2 = ALO.enter_fn("A2")
            qt_sb = [eA2(nc.sbuf_tensor(f"qt_sb{i}", [128, NH * 512], BF16))
                     for i in range(2)]
            wo_sb = eA2(nc.sbuf_tensor("wo_sb", [128, NH * D], BF16))
            jio = eA2(nc.sbuf_tensor("jio", [128, NKC], F32))
            qio = eA2(nc.sbuf_tensor("qio", [1, QW], F32))
            maskneg = eA2(nc.sbuf_tensor("maskneg_sb", [128, 4 * QW], BF16))
            s2vec = eA2(nc.sbuf_tensor("s2vec", [128, NH], F32))
            bias_blk = eA2(nc.sbuf_tensor("bias_blk", [128, NH * NKC], F32))
            rowq = eA2(nc.sbuf_tensor("rowq", [1, NH * QW], BF16))
            mt_sb = eA2(nc.sbuf_tensor("mt_sb", [128, 2 * QW], BF16))
            p_sb = eA2(nc.sbuf_tensor("p_sb", [128, 2 * NH * QW], BF16))
            pv_bf = eA2(nc.sbuf_tensor("pv_bf", [128, QW], BF16))
            att_tm = eA2(nc.sbuf_tensor("att_tm", [128, QW], BF16))
            attf = eA2(nc.sbuf_tensor("attf", [128, NH * QW], BF16))
            l_sb = eA2(nc.sbuf_tensor("l_sb", [128, QW], F32))
            linv = eA2(nc.sbuf_tensor("linv", [128, 4 * NH], F32))
            wo_out = eA2(nc.sbuf_tensor("wo_out", [128, D], F32))
            sq512 = eA2(nc.sbuf_tensor("sq512", [128, 512], BF16))
            krow = eA2(nc.sbuf_tensor("krow", [1, 512], F32))
            ghB = eA2(nc.sbuf_tensor("ghB", [1, NH], F32))

            if True:
                def g_loads2(g):
                    wait(g, "sout", cnt("sout"))
                    wait(g, "sdbg", cnt("sdbg"))
                    wait(g, "sqo0", cnt("sqo0"))
                    wait(g, "sqo1", cnt("sqo1"))
                    for hc in range(NH):
                        inc(g.dma_start(out=wo_sb[:, D * hc:D * hc + D],
                                        in_=dwo[128 * hc:128 * hc + 128, :]), "sgd", 16)
                    for r in range(4):
                        inc(g.dma_start(out=maskneg[:, QW * r:QW * r + QW],
                                        in_=dmask[128 * r:128 * r + 128, :]), "sgd", 16)
                blk.gpsimd(g_loads2)

                def s_loads2(s):
                    wait(s, "sout", cnt("sout"))
                    wait(s, "sdbg", cnt("sdbg"))
                    wait(s, "sqo0", cnt("sqo0"))
                    wait(s, "sqo1", cnt("sqo1"))
                    inc(s.dma_start(out=jio[:], in_=djio), "sx", 16)
                    inc(s.dma_start(out=qio[:], in_=dqio), "sx", 16)
                blk.sync(s_loads2)

                def v_rowq(v):
                    wait(v, "sgd", cnt("sgd"))
                    wait(v, "sx", cnt("sx"))
                    wait(v, "sout", cnt("sout"))
                    wait(v, "sdbg", cnt("sdbg"))
                    wait(v, "sqo0", cnt("sqo0"))
                    wait(v, "sqo1", cnt("sqo1"))
                    last = None
                    for h in range(NH):
                        last = v.tensor_scalar_mul(rowq[0:1, QW * h:QW * h + QW],
                                                   qio[0:1, :],
                                                   prm[0:1, 20 + h:21 + h])
                    inc(last, "sve")
                blk.vector(v_rowq)
            fence()

            if any(k in dbg for k in ("q", "k", "v", "a", "b")):
                if True:
                    def dmp(s):
                        wait(s, "sqo0", cnt("sqo0"))
                        wait(s, "sqo1", cnt("sqo1"))
                        if "q" in dbg:
                            inc(s.dma_start(out=dbg_outs["q"], in_=q_scr.ap()),
                                "sdbg", 16)
                        for name, srct in (("k", k_sb[:, :]), ("v", v_tm[:, :]),
                                           ("a", a_sb[32:64, :]),
                                           ("b", b_sb[32:64, :])):
                            if name in dbg:
                                inc(s.dma_start(out=dbg_outs[name], in_=srct),
                                    "sdbg", 16)
                    blk.gpsimd(dmp)
                fence()

            if True:
                def g_prep(g):
                    for ic in range(NIC):
                        gsrc = bass.AP(dgw.tensor, 128 * ic,
                                       [[II, 128], [128 * II, NCI], [1, 128]])
                        gdst = bass.AP(gw_scr, 128 * ic * D,
                                       [[D, 128], [128, NCI], [1, 128]])
                        inc(g.dma_start(out=gdst, in_=gsrc), "sprep", 16)
                        usrc = bass.AP(duw.tensor, 128 * ic,
                                       [[II, 128], [128 * II, NCI], [1, 128]])
                        udst = bass.AP(uw_scr, 128 * ic * D,
                                       [[D, 128], [128, NCI], [1, 128]])
                        inc(g.dma_start(out=udst, in_=usrc), "sprep", 16)
                    for ec in range(NEC):
                        dsrc = bass.AP(ddw.tensor, 128 * ec,
                                       [[D, 128], [128 * D, NIC], [1, 128]])
                        ddst = bass.AP(dw_scr, 128 * ec * II,
                                       [[II, 128], [128, NIC], [1, 128]])
                        inc(g.dma_start(out=ddst, in_=dsrc), "sprep", 16)
                blk.gpsimd(g_prep)

                def ve_z(v):
                    inc(v.memset(prm[0:1, 24:29], 0.0), "sve")
                blk.vector(ve_z)

                def norms_from(src_fn, dstcol, need_dma):
                    for s16 in range(16):
                        bnk = 2 + (s16 % 2)
                        if need_dma:
                            def sy_q(s, s16=s16, prev=cnt("sve"),
                                     q0=cnt("sqo0"), q1=cnt("sqo1")):
                                wait(s, "sve", prev)
                                wait(s, "sqo0", q0)
                                wait(s, "sqo1", q1)
                                inc(s.dma_start(out=qt_sb[0][:, 0:512],
                                                in_=src_fn(s16)), "sqt0", 16)
                            blk.sync(sy_q)
                            xr = cnt("sqt0")

                            def ve_sq(v, xr=xr):
                                wait(v, "sqt0", xr)
                                inc(v.tensor_mul(sq512[:], qt_sb[0][:, 0:512],
                                                 qt_sb[0][:, 0:512]), "sve")
                            blk.vector(ve_sq)
                        else:
                            def ve_sq(v, s16=s16, prev_pe=cnt("spe")):
                                wait(v, "spe", prev_pe)
                                inc(v.tensor_mul(sq512[:], src_fn(s16),
                                                 src_fn(s16)), "sve")
                            blk.vector(ve_sq)
                        sq_done = cnt("sve")

                        def pe_n(t_, bnk=bnk, sq_done=sq_done, prev_ev=cnt("sac")):
                            wait(t_, "sve", sq_done)
                            wait(t_, "sac", prev_ev)
                            inc(t_.matmul(bank(bnk, p0=0, p1=1), ones_bf[:],
                                          sq512[:], start=True, stop=True), "spe")
                        blk.tensor(pe_n)
                        mm_done = cnt("spe")

                        def act_n(s, bnk=bnk, mm_done=mm_done, prev_ve=cnt("sve")):
                            wait(s, "spe", mm_done)
                            wait(s, "sve", prev_ve)
                            inc(s.copy(krow[0:1, :], bank(bnk, p0=0, p1=1)), "sac")
                        blk.scalar(act_n)
                        ev_done = cnt("sac")

                        def ve_mx(v, dstcol=dstcol, ev_done=ev_done):
                            wait(v, "sac", ev_done)
                            inc(v.reduce_max(t11[0:1, 15:16], krow[0:1, :],
                                             axis=AX.X), "sve")
                            wait(v, "sve", cnt("sve"))
                            inc(v.tensor_max(prm[0:1, dstcol:dstcol + 1],
                                             prm[0:1, dstcol:dstcol + 1],
                                             t11[0:1, 15:16]), "sve")
                        blk.vector(ve_mx)

                norms_from(lambda s16: k_sb[:, 512 * s16:512 * s16 + 512], 28, False)
                for h in range(NH):
                    norms_from(lambda s16, h=h: q_scr.ap()[:, T * h + 512 * s16:
                                                           T * h + 512 * s16 + 512],
                               24 + h, True)

                def ve_ghb(v):
                    wait(v, "sve", cnt("sve"))
                    inc(v.tensor_scalar_add(t11[0:1, 0:4], prm[0:1, 24:28],
                                            prm[0:1, 28:29]), "sve")
                    wait(v, "sve", cnt("sve"))
                    inc(v.tensor_scalar(t11[0:1, 0:4], t11[0:1, 0:4], -0.5, -MARGIN,
                                        ALU.mult, ALU.add), "sve")
                    wait(v, "sve", cnt("sve"))
                    inc(v.tensor_add(ghB[0:1, 0:4], t11[0:1, 0:4], prm[0:1, 16:20]),
                        "sve")
                blk.vector(ve_ghb)

            fence()

            if True:
                b_sc = ((0, 1), (2, 3))
                exp_free = [0, 0, 0, 0]
                mt_cons = [0, 0]
                mtb_ac = 0
                mtb_ve = 0
                p_freed = [0, 0]
                l_evac = 0
                wo_prev = 0
                prev_s2b = 0
                qt_ready = [0, 0]
                qt_users = [0, 0]

                for t in range(NT):
                    nkc = 4 * (t + 1)
                    tb2 = t % 2

                    def sy_qt(s, t=t, tb2=tb2, qu=qt_users[tb2],
                              q0=cnt("sqo0"), q1=cnt("sqo1")):
                        wait(s, "spe", qu)
                        wait(s, "sqo0", q0)
                        wait(s, "sqo1", q1)
                        for h in range(NH):
                            inc(s.dma_start(out=qt_sb[tb2][:, 512 * h:512 * h + 512],
                                            in_=q_scr.ap()[:, T * h + 512 * t:
                                                           T * h + 512 * t + 512]),
                                f"sqt{tb2}", 16)
                    blk.sync(sy_qt)
                    qt_ready[tb2] = cnt(f"sqt{tb2}")

                    def ve_s2(v, t=t, prev_s2b=prev_s2b):
                        wait(v, "spe", prev_s2b)
                        last = None
                        for h in range(NH):
                            last = v.tensor_scalar(t11[0:1, h:h + 1],
                                                   prm[0:1, 12 + h:13 + h],
                                                   float(-512 * t), ghB[0:1, h:h + 1],
                                                   ALU.mult, ALU.add)
                        inc(last, "sve")
                    blk.vector(ve_s2)
                    s2_done = cnt("sve")

                    def pe_s2(t_, s2_done=s2_done, ma=mtb_ac, mv=mtb_ve,
                              wp=wo_prev):
                        wait(t_, "sve", s2_done)
                        wait(t_, "sac", ma)
                        wait(t_, "sve", mv)
                        wait(t_, "sac", wp)
                        last = None
                        for h in range(NH):
                            last = t_.matmul(bank(4, c0=16 + h, c1=17 + h),
                                             onesrow_f[:], t11[0:1, h:h + 1],
                                             start=True, stop=True)
                        inc(last, "spe")
                    blk.tensor(pe_s2)
                    s2b_done = prev_s2b = cnt("spe")

                    def act_s2(s, s2b_done=s2b_done):
                        wait(s, "spe", s2b_done)
                        inc(s.copy(s2vec[:], bank(4, c0=16, c1=20)), "sac")
                    blk.scalar(act_s2)
                    s2v_done = cnt("sac")
                    mtb_ac = s2v_done

                    def ve_bias(v, s2v_done=s2v_done):
                        wait(v, "sac", s2v_done)
                        last = None
                        for h in range(NH):
                            last = v.tensor_scalar(bias_blk[:, NKC * h:NKC * h + NKC],
                                                   jio[:], cvec[:, h:h + 1],
                                                   s2vec[:, h:h + 1],
                                                   ALU.mult, ALU.add)
                        inc(last, "sve")
                    blk.vector(ve_bias)

                    for hp in range(2):
                        exp_done_k = {}

                        def emit_pv(kc, hp=hp, t=t, nkc=nkc):
                            par = kc % 2
                            h0, h1 = 2 * hp, 2 * hp + 1

                            def pe_pv(t_, t=t, kc=kc, par=par, h0=h0, h1=h1,
                                      nkc=nkc, ex=exp_done_k[kc], wp=wo_prev,
                                      le=l_evac):
                                wait(t_, "sac", ex)
                                if kc == 0:
                                    wait(t_, "sac", wp)
                                    wait(t_, "sac", le)
                                st = kc == 0
                                sp = kc == nkc - 1
                                for pi, h in ((0, h0), (1, h1)):
                                    t_.matmul(bank(5 + pi),
                                              v_tm[:, 128 * kc:128 * kc + 128],
                                              p_sb[:, QW * (2 * par + pi):
                                                   QW * (2 * par + pi) + QW],
                                              start=st, stop=sp)
                                last = None
                                for pi, h in ((0, h0), (1, h1)):
                                    last = t_.matmul(
                                        bank(7, p0=32 * h, p1=32 * h + 1),
                                        ones_bf[:],
                                        p_sb[:, QW * (2 * par + pi):
                                             QW * (2 * par + pi) + QW],
                                        start=st, stop=sp,
                                        tile_position=(0, 32 * h))
                                inc(last, "spe")
                            blk.tensor(pe_pv)
                            p_freed[par] = cnt("spe")

                        for kc in range(nkc):
                            par = kc % 2
                            sb0, sb1 = b_sc[par]
                            h0, h1 = 2 * hp, 2 * hp + 1

                            def pe_mt(t_, t=t, kc=kc, ma=mtb_ac, mv=mtb_ve):
                                wait(t_, "sac", ma)
                                if kc == 0:
                                    wait(t_, "sve", mv)
                                inc(t_.matmul(bank(4),
                                              b_sb[32:64, 128 * kc:128 * kc + 128],
                                              a_sb[32:64, 512 * t:512 * t + 512],
                                              start=True, stop=True), "spe")
                            blk.tensor(pe_mt)
                            mtmm_done = cnt("spe")

                            def act_mt(s, par=par, mtmm_done=mtmm_done,
                                       mtc=mt_cons[par]):
                                wait(s, "spe", mtmm_done)
                                wait(s, "sve", mtc)
                                inc(s.activation(mt_sb[:, QW * par:QW * par + QW],
                                                 bank(4), AF.Tanh, scale=0.5), "sac")
                            blk.scalar(act_mt)
                            mtb_ac = tanh_done = cnt("sac")

                            mask_done = 0
                            diag_r = kc - 4 * t
                            if diag_r >= 0:
                                def ve_mask(v, par=par, diag_r=diag_r,
                                            tanh_done=tanh_done):
                                    wait(v, "sac", tanh_done)
                                    inc(v.tensor_add(
                                        mt_sb[:, QW * par:QW * par + QW],
                                        mt_sb[:, QW * par:QW * par + QW],
                                        maskneg[:, QW * diag_r:QW * diag_r + QW]),
                                        "sve")
                                blk.vector(ve_mask)
                                mask_done = cnt("sve")

                            def pe_qk(t_, t=t, kc=kc, h0=h0, h1=h1, sb0=sb0,
                                      sb1=sb1, e0=exp_free[sb0], e1=exp_free[sb1],
                                      tb2=tb2, qr=qt_ready[tb2]):
                                wait(t_, "sac", max(e0, e1))
                                if kc < 2:
                                    wait(t_, f"sqt{tb2}", qr)
                                for bi, h in ((sb0, h0), (sb1, h1)):
                                    t_.matmul(bank(bi),
                                              k_sb[:, 128 * kc:128 * kc + 128],
                                              qt_sb[tb2][:, 512 * h:512 * h + 512],
                                              start=True, stop=False)
                                last = None
                                for bi, h in ((sb0, h0), (sb1, h1)):
                                    last = t_.matmul(bank(bi), onesrow_bf[:],
                                                     rowq[0:1, QW * h:QW * h + QW],
                                                     start=False, stop=True)
                                inc(last, "spe")
                            blk.tensor(pe_qk)
                            qk_done = cnt("spe")

                            def ve_stt(v, par=par, h0=h0, h1=h1, sb0=sb0, sb1=sb1,
                                       qk_done=qk_done, mra=tanh_done,
                                       mrv=mask_done):
                                wait(v, "spe", qk_done)
                                wait(v, "sac", mra)
                                wait(v, "sve", mrv)
                                last = None
                                for bi, h in ((sb0, h0), (sb1, h1)):
                                    last = v.scalar_tensor_tensor(
                                        bank(bi), mt_sb[:, QW * par:QW * par + QW],
                                        ghvec[:, h:h + 1], bank(bi),
                                        ALU.mult, ALU.add)
                                inc(last, "sve")
                            blk.vector(ve_stt)
                            stt_done = cnt("sve")
                            mt_cons[par] = stt_done

                            def act_exp(s, par=par, h0=h0, h1=h1, sb0=sb0, sb1=sb1,
                                        kc=kc, stt_done=stt_done, pf=p_freed[par]):
                                wait(s, "sve", stt_done)
                                wait(s, "spe", pf)
                                last = None
                                for bi, (pi, h) in ((sb0, (0, h0)), (sb1, (1, h1))):
                                    last = s.activation(
                                        p_sb[:, QW * (2 * par + pi):
                                             QW * (2 * par + pi) + QW],
                                        bank(bi), AF.Exp,
                                        bias=bias_blk[:, NKC * h + kc:
                                                      NKC * h + kc + 1])
                                inc(last, "sac")
                            blk.scalar(act_exp)
                            exp_free[sb0] = exp_free[sb1] = cnt("sac")
                            exp_done_k[kc] = cnt("sac")
                            if kc >= 1:
                                emit_pv(kc - 1)
                        emit_pv(nkc - 1)

                        pv_done = cnt("spe")
                        prev_pe = pv_done

                        for hi in range(2):
                            h = 2 * hp + hi

                            def act_ev1(s, h=h, hi=hi, prev_pe=prev_pe):
                                wait(s, "spe", prev_pe)
                                s.copy(pv_bf[:], bank(5 + hi))
                                inc(s.copy(l_sb[32 * h:32 * h + 1, :],
                                           bank(7, p0=32 * h, p1=32 * h + 1)),
                                    "sac")
                            blk.scalar(act_ev1)
                            ev1_done = cnt("sac")
                            l_evac = ev1_done

                            def pe_lt(t_, h=h, ev1_done=ev1_done, mv=mtb_ve,
                                      e0=exp_free[0]):
                                wait(t_, "sac", ev1_done)
                                wait(t_, "sve", mv)
                                wait(t_, "sac", e0)
                                for qc in range(4):
                                    t_.matmul(bank(4, c0=4 * h + qc,
                                                   c1=4 * h + qc + 1),
                                              l_sb[32 * h:32 * h + 1,
                                                   128 * qc:128 * qc + 128],
                                              onesf[32 * h:32 * h + 1, :],
                                              start=True, stop=True,
                                              tile_position=(32 * h, 0))
                                last = None
                                for qc in range(4):
                                    last = t_.matmul(
                                        bank(0, c0=128 * qc, c1=128 * qc + 128),
                                        pv_bf[:, 128 * qc:128 * qc + 128],
                                        ident[:], start=True, stop=True)
                                inc(last, "spe")
                            blk.tensor(pe_lt)
                            lt_done = prev_pe = cnt("spe")

                            def ve_linv(v, h=h, lt_done=lt_done):
                                wait(v, "spe", lt_done)
                                inc(v.reciprocal(linv[:, 4 * h:4 * h + 4],
                                                 bank(4, c0=4 * h, c1=4 * h + 4)),
                                    "sve")
                            blk.vector(ve_linv)
                            linv_done = cnt("sve")
                            mtb_ve = linv_done

                            def act_tm(s, h=h, linv_done=linv_done):
                                wait(s, "sve", linv_done)
                                last = None
                                for qc in range(4):
                                    last = s.activation(
                                        att_tm[:, 128 * qc:128 * qc + 128],
                                        bank(0, c0=128 * qc, c1=128 * qc + 128),
                                        AF.Copy,
                                        scale=linv[:, 4 * h + qc:4 * h + qc + 1])
                                inc(last, "sac")
                            blk.scalar(act_tm)
                            tm_done = cnt("sac")
                            exp_free[0] = tm_done

                            def pe_fm(t_, h=h, tm_done=tm_done, e1=exp_free[1]):
                                wait(t_, "sac", tm_done)
                                wait(t_, "sac", e1)
                                last = None
                                for qc in range(4):
                                    last = t_.matmul(
                                        bank(1, c0=128 * qc, c1=128 * qc + 128),
                                        att_tm[:, 128 * qc:128 * qc + 128],
                                        ident[:], start=True, stop=True)
                                inc(last, "spe")
                            blk.tensor(pe_fm)
                            fm_done = prev_pe = cnt("spe")

                            def act_fm(s, h=h, fm_done=fm_done):
                                wait(s, "spe", fm_done)
                                inc(s.copy(attf[:, QW * h:QW * h + QW], bank(1)),
                                    "sac")
                            blk.scalar(act_fm)
                            exp_free[1] = cnt("sac")
                    attf_done = cnt("sac")

                    if "attf" in dbg and t == 1:
                        def dmpa(s):
                            wait(s, "sac", attf_done)
                            inc(s.dma_start(out=dbg_outs["attf"], in_=attf[:]),
                                "sdbg", 16)
                            inc(s.dma_start(out=dbg_outs["lrow"], in_=l_sb[:]),
                                "sdbg", 16)
                        blk.gpsimd(dmpa)

                    for b4 in range(4):
                        def pe_wo(t_, t=t, b4=b4, attf_done=attf_done,
                                  wp=wo_prev, pev=cnt("sac"), mv=mtb_ve):
                            wait(t_, "sac", attf_done)
                            wait(t_, "sac", pev)
                            wait(t_, "sve", mv)
                            if b4 == 0:
                                wait(t_, "sac", wp)
                            last = None
                            for h in range(NH):
                                for n in range(4):
                                    last = t_.matmul(bank(4 + n),
                                                     attf[:, QW * h + 128 * b4:
                                                          QW * h + 128 * b4 + 128],
                                                     wo_sb[:, D * h + 512 * n:
                                                           D * h + 512 * n + 512],
                                                     start=(h == 0),
                                                     stop=(h == NH - 1))
                            inc(last, "spe")
                        blk.tensor(pe_wo)
                        wo_done = cnt("spe")

                        def act_wo(s, wo_done=wo_done, prev_st=cnt("srs")):
                            wait(s, "spe", wo_done)
                            wait(s, "srs", prev_st)
                            inc(s.copy(wo_out[:], pp[:, 2048:4096]), "sac")
                        blk.scalar(act_wo)
                        woe_done = cnt("sac")

                        def sy_wo(s, t=t, b4=b4, woe_done=woe_done):
                            wait(s, "sac", woe_done)
                            inc(s.dma_start(out=rs_in[512 * t + 128 * b4:
                                                      512 * t + 128 * b4 + 128, :],
                                            in_=wo_out[:]), "srs", 16)
                        blk.sync(sy_wo)
                    wo_prev = cnt("sac")
                    mtb_ac = wo_prev
                    qt_users[tb2] = cnt("spe")


                def g_rs(g):
                    wait(g, "srs", cnt("srs"))
                    inc(g.collective_compute(
                        "ReduceScatter", ALU.add,
                        replica_groups=[[0, 1, 2, 3], [4, 5, 6, 7]],
                        ins=[rs_in.ap()], outs=[rs_out.ap()]), "scc")
                blk.gpsimd(g_rs)

                if "woin" in dbg:
                    def dmpw(s):
                        wait(s, "srs", cnt("srs"))
                        inc(s.dma_start(out=dbg_outs["woin"], in_=rs_in.ap()),
                            "sdbg", 16)
                    blk.sync(dmpw)

            fence()
            ALO.close("A2")
            ALO.close("A")

            eB = ALO.enter_fn("B")
            xn_own = eB(nc.sbuf_tensor("xn_own", [128, NCI * TQ], BF16))
            gs_sb = eB(nc.sbuf_tensor("gs_sb", [128, NIC], F32))
            us_sb = eB(nc.sbuf_tensor("us_sb", [128, NIC], F32))
            ds_sb = eB(nc.sbuf_tensor("ds_sb", [128, NEC], F32))
            dn_fm = [eB(nc.sbuf_tensor(f"dn_fm{i}", [128, 1024], F32))
                     for i in range(2)]
            dn_ev = [eB(nc.sbuf_tensor(f"dn_ev{i}", [128, 128], F32))
                     for i in range(2)]
            gw_sb = [eB(nc.sbuf_tensor(f"gw_sb{i}", [128, D], FP8))
                     for i in range(2)]
            uw_sb = [eB(nc.sbuf_tensor(f"uw_sb{i}", [128, D], FP8))
                     for i in range(2)]
            gate_bf2 = [eB(nc.sbuf_tensor(f"gate_bf{i}", [128, 512], BF16))
                        for i in range(2)]
            eP4 = ALO.enter_fn("P4")
            xblk5 = [eP4(nc.sbuf_tensor(f"xblk5_{i}", [128, D], F32)) for i in range(4)]
            xn_tm5 = eP4(nc.sbuf_tensor("xn_tm5", [128, 4 * D], BF16))
            stat5 = eP4(nc.sbuf_tensor("stat5", [128, 20], F32))

            if True:
                def sy_sc(s):
                    wait(s, "sout", cnt("sout"))
                    wait(s, "sdbg", cnt("sdbg"))
                    wait(s, "sqt0", cnt("sqt0"))
                    wait(s, "sqt1", cnt("sqt1"))
                    inc(s.dma_start(out=gs_sb[:], in_=dgs), "sx", 16)
                    inc(s.dma_start(out=us_sb[:], in_=dus), "sx", 16)
                    inc(s.dma_start(out=ds_sb[:], in_=dds), "sx", 16)
                blk.sync(sy_sc)
                sc_ready = cnt("sx")

                def g_guard(g):
                    wait(g, "sout", cnt("sout"))
                    wait(g, "sdbg", cnt("sdbg"))
                    wait(g, "sqt0", cnt("sqt0"))
                    wait(g, "sqt1", cnt("sqt1"))
                blk.gpsimd(g_guard)

                def v_guard(v):
                    wait(v, "sout", cnt("sout"))
                    wait(v, "sdbg", cnt("sdbg"))
                    wait(v, "sqt0", cnt("sqt0"))
                    wait(v, "sqt1", cnt("sqt1"))
                blk.vector(v_guard)

                def a_guard(s):
                    wait(s, "sout", cnt("sout"))
                    wait(s, "sdbg", cnt("sdbg"))
                    wait(s, "sqt0", cnt("sqt0"))
                    wait(s, "sqt1", cnt("sqt1"))
                blk.scalar(a_guard)

                emit_ln_phase(blk, dxo, 4,
                              lambda ci, t: xn_own[:, TQ * ci + 512 * t:
                                                   TQ * ci + 512 * t + 512],
                              xblk5, xn_tm5, stat5, None)
                xn_own_done = cnt("sac")
            fence()
            ALO.close("P4")
            fuse_sb = eB(nc.sbuf_tensor("fuse_sb", [128, NIC * 512], BF16))
            dw_sb = [eB(nc.sbuf_tensor(f"dw_sb{i}", [128, NIC * 128], FP8))
                     for i in range(2)]
            fin = [eB(nc.sbuf_tensor(f"fin{i}", [128, D], F32)) for i in range(2)]

            if True:

                if "xnown" in dbg:
                    def dmpx(s):
                        wait(s, "sac", xn_own_done)
                        inc(s.dma_start(out=dbg_outs["xnown"], in_=xn_own[:]),
                            "sdbg", 16)
                    blk.gpsimd(dmpx)

                w_free = [0, 0]
                dwf = [0, 0]
                gu_bank_free = [[0, 0], [0, 0]]
                dn_bank_free = [0, 0]
                tr_bank_free = [0, 0]
                trctr = 0
                for ts in range(4):
                    for ic in range(NIC):
                        bi = ic % 2

                        def g_w(g, ic=ic, bi=bi, wf=w_free[bi],
                                pr=cnt("sprep")):
                            wait(g, "spe", wf)
                            wait(g, "sprep", pr)
                            inc(g.dma_start(out=gw_sb[bi][:, :],
                                            in_=gw_scr.ap()[128 * ic:
                                                            128 * ic + 128, :]),
                                f"sw5{bi}", 16)
                            inc(g.dma_start(out=uw_sb[bi][:, :],
                                            in_=uw_scr.ap()[128 * ic:
                                                            128 * ic + 128, :]),
                                f"sw5{bi}", 16)
                        blk.sync(g_w)
                        w_ready = cnt(f"sw5{bi}")

                        ip = ic % 2
                        gb = 4 + 2 * ip

                        def pe_gu(t_, ts=ts, bi=bi, gb=gb, w_ready=w_ready,
                                  xn_done=xn_own_done,
                                  prev_ev=gu_bank_free[ip][0],
                                  prev_fu=gu_bank_free[ip][1], scr=sc_ready):
                            wait(t_, f"sw5{bi}", w_ready)
                            wait(t_, "sx", scr)
                            wait(t_, "sac", xn_done)
                            wait(t_, "sac", prev_ev)
                            wait(t_, "sve", prev_fu)
                            last = None
                            for cc in range(NCI):
                                rhs = xn_own[:, TQ * cc + 512 * ts:
                                             TQ * cc + 512 * ts + 512]
                                t_.matmul(bank(gb),
                                          gw_sb[bi][:, 128 * cc:128 * cc + 128],
                                          rhs, start=(cc == 0),
                                          stop=(cc == NCI - 1))
                                last = t_.matmul(
                                    bank(gb + 1),
                                    uw_sb[bi][:, 128 * cc:128 * cc + 128],
                                    rhs, start=(cc == 0), stop=(cc == NCI - 1))
                            inc(last, "spe")
                        blk.tensor(pe_gu)
                        gu_done = cnt("spe")
                        w_free[bi] = gu_done

                        def act_gate(s, ic=ic, ip=ip, gb=gb, gu_done=gu_done):
                            wait(s, "spe", gu_done)
                            inc(s.activation(gate_bf2[ip][:], bank(gb), AF.Silu,
                                             scale=gs_sb[:, ic:ic + 1]), "sac")
                        blk.scalar(act_gate)
                        gate_done = cnt("sac")

                        def ve_fuse(v, ic=ic, ip=ip, gb=gb, gate_done=gate_done):
                            wait(v, "sac", gate_done)
                            inc(v.scalar_tensor_tensor(
                                fuse_sb[:, 512 * ic:512 * ic + 512],
                                bank(gb + 1), us_sb[:, ic:ic + 1],
                                gate_bf2[ip][:],
                                ALU.mult, ALU.mult), "sve")
                        blk.vector(ve_fuse)
                        gu_bank_free[ip] = [cnt("sac"), cnt("sve")]
                    fuse_done = cnt("sve")

                    for ec in range(NEC):
                        bi = ec % 2

                        def g_dw(g, ec=ec, bi=bi, prev=dwf[bi],
                                 pr=cnt("sprep")):
                            wait(g, "spe", prev)
                            wait(g, "sprep", pr)
                            inc(g.dma_start(out=dw_sb[bi][:, :],
                                            in_=dw_scr.ap()[128 * ec:
                                                            128 * ec + 128, :]),
                                f"sdw{bi}", 16)
                        blk.sync(g_dw)
                        dw_ready = cnt(f"sdw{bi}")

                        ep = ec % 2
                        db = 2 * ep

                        def pe_dn(t_, bi=bi, ep=ep, db=db, dw_ready=dw_ready,
                                  fuse_done=fuse_done,
                                  prev_ev=dn_bank_free[ep]):
                            wait(t_, f"sdw{bi}", dw_ready)
                            wait(t_, "sve", fuse_done)
                            wait(t_, "sac", prev_ev)
                            last = None
                            for ic in range(NIC):
                                last = t_.matmul(
                                    bank(db),
                                    dw_sb[bi][:, 128 * ic:128 * ic + 128],
                                    fuse_sb[:, 512 * ic:512 * ic + 512],
                                    start=(ic == 0), stop=(ic == NIC - 1))
                            inc(last, "spe")
                        blk.tensor(pe_dn)
                        dn_done = cnt("spe")
                        dwf[bi] = dn_done

                        def act_dn(s, ec=ec, ep=ep, db=db, dn_done=dn_done):
                            wait(s, "spe", dn_done)
                            inc(s.activation(dn_fm[ep][:, 0:512], bank(db),
                                             AF.Copy,
                                             scale=ds_sb[:, ec:ec + 1]), "sac")
                        blk.scalar(act_dn)
                        dnf_done = cnt("sac")
                        dn_bank_free[ep] = dnf_done

                        for tb8 in range(4):
                            tb_ = 6 + trctr % 2
                            tslot = (trctr // 2) % 2
                            trctr += 1

                            def pe_tr5(t_, ep=ep, tb8=tb8, tb_=tb_, tslot=tslot,
                                       dnf_done=dnf_done,
                                       prev_ev=tr_bank_free[tb_ - 6]):
                                wait(t_, "sac", dnf_done)
                                wait(t_, "sac", prev_ev)
                                inc(t_.matmul(bank(tb_, c0=128 * tslot,
                                                  c1=128 * tslot + 128),
                                              dn_fm[ep][:, 128 * tb8:
                                                        128 * tb8 + 128],
                                              identf[:], start=True, stop=True),
                                    "spe")
                            blk.tensor(pe_tr5)
                            tr_done = cnt("spe")

                            def act_tr5(s, tb_=tb_, tslot=tslot, tb8=tb8,
                                        tr_done=tr_done, prev_st=cnt("srs")):
                                wait(s, "spe", tr_done)
                                wait(s, "srs", prev_st)
                                inc(s.copy(dn_ev[tb8 % 2][:],
                                           bank(tb_, c0=128 * tslot,
                                                c1=128 * tslot + 128)), "sac")
                            blk.scalar(act_tr5)
                            ev_done = cnt("sac")
                            tr_bank_free[tb_ - 6] = ev_done

                            def sy_tr5(s, ts=ts, ec=ec, tb8=tb8, ev_done=ev_done):
                                wait(s, "sac", ev_done)
                                inc(s.dma_start(
                                    out=dn_scr.ap()[512 * ts + 128 * tb8:
                                                    512 * ts + 128 * tb8 + 128,
                                                    128 * ec:128 * ec + 128],
                                    in_=dn_ev[tb8 % 2][:]), "srs", 16)
                            blk.sync(sy_tr5)
                    gu_bank_free[0] = [cnt("sac"), gu_bank_free[0][1]]
                    gu_bank_free[1] = [cnt("sac"), gu_bank_free[1][1]]

                dn_all = cnt("srs")
                for tb in range(TQ // 128):
                    def sy_fin(s, tb=tb, dn_all=dn_all, prev=cnt("sve"),
                               prev_out=cnt("sout")):
                        wait(s, "srs", dn_all)
                        wait(s, "scc", 1)
                        wait(s, "sve", prev)
                        wait(s, "sout", prev_out)
                        inc(s.dma_start(out=fin[0][:],
                                        in_=dxo[128 * tb:128 * tb + 128, :]), "sx", 16)
                        inc(s.dma_start(out=fin[1][:],
                                        in_=rs_out.ap()[128 * tb:128 * tb + 128, :]),
                            "sx", 16)
                    blk.sync(sy_fin)
                    f_ready = cnt("sx")

                    def ve_fin(v, f_ready=f_ready):
                        wait(v, "sx", f_ready)
                        inc(v.tensor_add(fin[0][:], fin[0][:], fin[1][:]), "sve")
                    blk.vector(ve_fin)
                    add1 = cnt("sve")

                    def sy_fin2(s, tb=tb, add1=add1):
                        wait(s, "sve", add1)
                        inc(s.dma_start(out=fin[1][:],
                                        in_=dn_scr.ap()[128 * tb:128 * tb + 128, :]),
                            "sx", 16)
                    blk.sync(sy_fin2)
                    f2_ready = cnt("sx")

                    def ve_fin2(v, f2_ready=f2_ready):
                        wait(v, "sx", f2_ready)
                        inc(v.tensor_add(fin[0][:], fin[0][:], fin[1][:]), "sve")
                    blk.vector(ve_fin2)
                    fin_done = cnt("sve")

                    def sy_out(s, tb=tb, fin_done=fin_done):
                        wait(s, "sve", fin_done)
                        inc(s.dma_start(out=dout[128 * tb:128 * tb + 128, :],
                                        in_=fin[0][:]), "sout", 16)
                    blk.sync(sy_out)

                def g_end(g):
                    wait(g, "sout", cnt("sout"))
                    wait(g, "sdbg", cnt("sdbg"))
                blk.gpsimd(g_end)

            ALO.close("B")

            _chk = {n: v[1] for n, v in sems.items()}
            if _finals is None:
                _finals = _chk
            else:
                assert _finals == _chk, (_live, _finals, _chk)
        _real_blk.__exit__(None, None, None)


    return nc, dbg_outs



def _consts():
    inv = (10000.0 ** (-np.arange(64, dtype=np.float64) / 64.0))
    ang = np.arange(T, dtype=np.float64)[None, :] * inv[:, None]
    cos64 = np.cos(ang) * RSC
    sin64 = np.sin(ang) * RSC
    cos_t = np.concatenate([cos64, cos64], 0).astype(np.float32)
    sin_t = np.concatenate([-sin64, sin64], 0).astype(np.float32)
    ident = np.eye(128, dtype=np.float32)
    jio = (np.arange(NKC, dtype=np.float32)[None, :] * 128.0
           + np.arange(128, dtype=np.float32)[:, None])
    qio = np.arange(QW, dtype=np.float32)[None, :]
    mask = np.zeros((4 * 128, QW), dtype=np.float32)
    j = np.arange(128)[:, None]
    q = np.arange(QW)[None, :]
    for r in range(4):
        mask[128 * r:128 * r + 128] = np.where(q >= j + 128 * r, 0.0, NEG)
    return cos_t, sin_t, ident, jio, qio, mask


def make_in_maps(inputs):
    x = np.ascontiguousarray(np.asarray(inputs["L0_attn_x"], dtype=np.float32))
    wq = np.asarray(inputs["L0_attn_Wq"], dtype=np.float32)
    wk = np.asarray(inputs["L0_attn_Wk"], dtype=np.float32)
    wv = np.asarray(inputs["L0_attn_Wv"], dtype=np.float32)
    wo = np.asarray(inputs["L0_attn_Wo"], dtype=np.float32)
    wa = np.ascontiguousarray(np.asarray(inputs["L0_attn_pol_WA"], dtype=np.float32))
    wb = np.ascontiguousarray(np.asarray(inputs["L0_attn_pol_WB"], dtype=np.float32))
    pol_dir = np.asarray(inputs["L0_attn_pol_dir"], dtype=np.float32)
    pol_gate = np.asarray(inputs["L0_attn_pol_gate"], dtype=np.float32)
    gtp = np.asarray(inputs["L0_attn_gtp_gamma"], dtype=np.float32)
    gw = np.ascontiguousarray(np.asarray(inputs["L0_ffn_gate_w"], dtype=np.float32))
    uw = np.ascontiguousarray(np.asarray(inputs["L0_ffn_up_w"], dtype=np.float32))
    dwn = np.ascontiguousarray(np.asarray(inputs["L0_ffn_down_w"], dtype=np.float32))
    gs = np.asarray(inputs["L0_ffn_gate_scale"], dtype=np.float32)
    us = np.asarray(inputs["L0_ffn_up_scale"], dtype=np.float32)
    ds = np.asarray(inputs["L0_ffn_down_scale"], dtype=np.float32)

    cos_t, sin_t, ident, jio, qio, mask = _consts()
    gs_r = np.ascontiguousarray(gs.reshape(NIC, 128).T)
    us_r = np.ascontiguousarray(us.reshape(NIC, 128).T)
    ds_r = np.ascontiguousarray(ds.reshape(NEC, 128).T)

    in_maps = []
    for c in range(8):
        b, g = c // 4, c % 4
        pol3 = np.concatenate([pol_dir[4 * g:4 * g + 4],
                               pol_gate[4 * g:4 * g + 4],
                               gtp[4 * g:4 * g + 4]]).astype(np.float32).reshape(1, 12)
        in_maps.append({
            "x_full": x[b],
            "x_own": np.ascontiguousarray(x[b, TQ * g:TQ * g + TQ]),
            "wq": np.ascontiguousarray(wq[:, 512 * g:512 * g + 512]),
            "wk": np.ascontiguousarray(wk[:, 128 * g:128 * g + 128]),
            "wv": np.ascontiguousarray(wv[:, 128 * g:128 * g + 128]),
            "wa": wa, "wb": wb,
            "wo_my": np.ascontiguousarray(wo[512 * g:512 * g + 512, :]),
            "gate_w": gw, "up_w": uw, "down_w": dwn,
            "gs_r": gs_r, "us_r": us_r, "ds_r": ds_r,
            "pol3": pol3,
            "cos_t": cos_t, "sin_t": sin_t, "ident": ident,
            "jiota": jio, "qiota": qio, "maskneg": mask,
        })
    return in_maps


def kernel(**inputs):
    in_maps = make_in_maps(inputs)
    nc, _ = build()
    res = run_bass_kernel_spmd(nc, in_maps, core_ids=list(range(8)))
    out = np.empty((2, T, D), dtype=np.float32)
    for c in range(8):
        b, g = c // 4, c % 4
        out[b, TQ * g:TQ * g + TQ] = res.results[c]["out"]
    return out
